# version 24
# baseline (speedup 1.0000x reference)
import types
import numpy as np
import ml_dtypes
from contextlib import ExitStack

import concourse.bass as bass
import concourse.mybir as mybir
from concourse.bass_utils import run_bass_kernel_spmd
from concourse.alu_op_type import AluOpType as ALU

F32 = mybir.dt.float32
BF16 = mybir.dt.bfloat16
AF = mybir.ActivationFunctionType
AX = mybir.AxisListType

L = 4096
D = 1024
NH = 16
NT = 32
NSTRIP = 8
SCALE = float(1.0 / np.sqrt(np.sqrt(64.0)))
EPS = 1e-5
R1024 = 1.0 / 1024.0


class _Fake:
    def __getattr__(self, k):
        return lambda *a, **kw: _FAKE

    def __getitem__(self, k):
        return _FAKE


_FAKE = _Fake()


def _isfake(x):
    return isinstance(x, _Fake)


class Ctr:
    def __init__(self, name, sem):
        self.name = name
        self.sem = sem
        self.n = 0

    def inc(self, instr, by=1):
        if not _isfake(instr):
            instr.then_inc(self.sem, by)
        self.n += by
        return self.n


class Em:
    def __init__(self, ctrs, milestones, dry):
        self.c = ctrs
        self.M = milestones
        self.dry = dry
        self._waited = {}

    def mark(self, key, ctr):
        if self.dry:
            self.M[key] = ctr.n
        else:
            assert self.M.get(key) == ctr.n, (key, self.M.get(key), ctr.n)

    def wait(self, eng, ename, ctr, key=None, val=None):
        if self.dry:
            return
        v = self.M[key] if key is not None else val
        if v is None or v <= 0:
            return
        prev = self._waited.get((ename, ctr.name), 0)
        if v <= prev:
            return
        self._waited[(ename, ctr.name)] = v
        eng.wait_ge(ctr.sem, v)

    def selfwait(self, eng, ename, ctr):
        self.wait(eng, ename, ctr, val=ctr.n)

    def fence_capture(self):
        return {k: c.n for k, c in self.c.items()}

    def fence_wait(self, eng, ename, fence):
        for k, v in fence.items():
            if v and k in self.c:
                self.wait(eng, ename, self.c[k], val=v)


def _r3(t, groups, part=None):
    if _isfake(t):
        return t
    ap = t.ap() if hasattr(t, "ap") and not isinstance(t, bass.AP) else t
    ap = ap.rearrange("p (c u) -> p c u", c=groups)
    if part is not None:
        ap = ap[part[0]:part[1]]
    return ap


def build_program():
    nc = bass.Bass("TRN2", target_bir_lowering=False)

    x_h = nc.dram_tensor("x", [L, D], F32, kind="ExternalInput").ap()
    wqt_h = nc.dram_tensor("wqt", [D, D], F32, kind="ExternalInput").ap()
    wkt_h = nc.dram_tensor("wkt", [D, D], F32, kind="ExternalInput").ap()
    wvt_h = nc.dram_tensor("wvt", [D, D], F32, kind="ExternalInput").ap()
    wft_h = nc.dram_tensor("wft", [D, D], F32, kind="ExternalInput").ap()
    bqr_h = nc.dram_tensor("bqr", [128, 8], F32, kind="ExternalInput").ap()
    bkr_h = nc.dram_tensor("bkr", [128, 8], F32, kind="ExternalInput").ap()
    bvb_h = nc.dram_tensor("bvb", [128, D], F32, kind="ExternalInput").ap()
    d2_h = nc.dram_tensor("d2", [128, D], F32, kind="ExternalInput").ap()
    eye_h = nc.dram_tensor("eye", [128, 128], BF16, kind="ExternalInput").ap()
    e7_h = nc.dram_tensor("e7", [128, 128], BF16, kind="ExternalInput").ap()
    e15_h = nc.dram_tensor("e15", [128, 128], BF16, kind="ExternalInput").ap()
    e13_h = nc.dram_tensor("e13", [128, 128], BF16, kind="ExternalInput").ap()
    pool_h = nc.dram_tensor("pool", [128, 4], BF16, kind="ExternalInput").ap()
    ones_h = nc.dram_tensor("ones1", [128, 1], BF16, kind="ExternalInput").ap()
    out_h = nc.dram_tensor("out", [L, D], F32, kind="ExternalOutput").ap()
    cc_in = nc.dram_tensor("cc_in", [1, 1], F32)
    cc_out = nc.dram_tensor("cc_out", [1, 1], F32)
    cc_w0 = nc.dram_tensor("cc_w0", [1, 1], F32)
    cc_w1 = nc.dram_tensor("cc_w1", [1, 1], F32)

    es = ExitStack()
    ctrs = {}

    def mksem(name):
        c = Ctr(name, es.enter_context(nc.semaphore(name)))
        ctrs[name] = c
        return c

    sg = mksem("sg")
    ss = mksem("ss")
    st = mksem("st")
    sv = mksem("sv")
    sa = mksem("sa")
    scc = mksem("scc")
    so = mksem("so")
    sp = mksem("sp")

    M = {}
    H = types.SimpleNamespace()

    OUTER = [("warm0", [1, 1], F32), ("qt", [128, 8 * L], BF16), ("eye", [128, 128], BF16),
             ("e7", [128, 128], BF16), ("e15", [128, 128], BF16),
             ("e13", [128, 128], BF16), ("pool", [128, 4], BF16),
             ("ones1", [128, 1], BF16), ("bq_r", [128, 8], F32),
             ("bk_r", [128, 8], F32), ("bqs", [128, 8], F32),
             ("bks", [128, 8], F32), ("qlm", [128, 1024], BF16),
             ("klm", [128, 1024], BF16), ("qlmp", [128, 2048], BF16),
             ("klmp", [128, 2048], BF16), ("w2aug", [128, NH * 65], BF16),
             ("vt0", [128, 2048], BF16), ("vt1", [128, 2048], BF16),
             ("bnorm", [128, 1024], BF16), ("gmax", [128, 1], F32),
             ("rg", [128, 1], F32), ("lmax", [1, 1], F32),
             ("cs_sb", [1, 4], F32), ("d_sb", [128, 16], F32),
             ("recd", [128, 16], F32)]
    CTX1 = [("kt", [128, 8 * L], BF16), ("xts", [128, 4096], BF16),
            ("xbf", [128, 4096], BF16), ("px", [128, 1024], BF16)]
    CTX2 = [("wq_sb", [128, 8 * D], BF16), ("wk_sb", [128, 8 * D], BF16)]
    PS_A = [("xtps0", [128, 1024], BF16), ("xtps1", [128, 1024], BF16),
            ("pxps", [128, 32], F32), ("qkps0", [128, 512], F32),
            ("qkps1", [128, 512], F32)]
    CTXB = [("k2sb", [128, 2048], BF16), ("k2t", [128, 2048], BF16),
            ("vc0", [128, 2048], BF16), ("vc1", [128, 2048], BF16),
            ("mtsb", [128, 2048], BF16), ("a1sb", [128, 2048], BF16),
            ("a2sb", [128, 2048], BF16), ("a3sb", [128, 2048], BF16),
            ("negmx2", [128, 16], F32), ("k2sum", [128, 16], F32),
            ("k2rec", [128, 16], F32)]
    PS_B = [("ps8", [128, 4096], F32)]
    CTXC = [("wv_sb", [128, 8 * D], BF16), ("vsb", [128, 2048], BF16),
            ("k3e", [128, 4096], BF16), ("bvb_sb", [128, D], F32)]
    PS_C = [("vps", [128, 512], F32), ("k3ps", [128, 1024], F32),
            ("bps0", [128, 512], F32), ("bps1", [128, 512], F32),
            ("dps", [128, 16], F32)]
    CTXD = [("wft_sb", [128, 8 * D], BF16), ("d2_sb", [128, D], F32),
            ("wtmp", [128, D], F32), ("x32", [128, 2 * D], F32),
            ("k1e", [128, 3 * NH * 256], BF16),
            ("xn", [128, 2 * D], F32),
            ("r_sb", [128, 3 * D], F32), ("y32", [128, 3 * D], F32),
            ("y16", [128, 2 * D], BF16), ("yt_sb", [128, 2 * D], BF16),
            ("r2_sb", [128, 2 * D], F32), ("sq_sb", [128, D], F32),
            ("outst", [128, 2 * D], F32), ("recden", [128, 2 * 32], F32),
            ("stats", [128, 2 * 12], F32), ("mv1", [128, 2 * 2], F32),
            ("negm1", [128, 2], F32), ("veps1", [128, 2], F32),
            ("rstd1", [128, 2], F32),
            ("lnv", [128, 2], F32), ("s2", [128, 2 * 2], F32),
            ("s2t", [128, 2], F32), ("m2", [128, 2], F32),
            ("m2sq", [128, 2], F32), ("e2", [128, 2], F32),
            ("veps2", [128, 2], F32), ("negm2", [128, 2], F32),
            ("rstd2", [128, 2], F32), ("ssq", [128, 2], F32)]
    PS_D = [("k1psA", [128, 512], F32), ("k1psB", [128, 512], F32),
            ("xq0", [128, 390], F32), ("xq1", [128, 390], F32),
            ("xq2", [128, 260], F32),
            ("ytps", [128, 1024], BF16), ("zps", [128, 1024], F32)]

    ALL = OUTER + CTX1 + CTX2 + PS_A + CTXB + PS_B + CTXC + PS_C + CTXD + PS_D
    for nm, _, _ in ALL:
        setattr(H, nm, _FAKE)

    def alloc(stk, specs, psum=False):
        for nm, shape, dt in specs:
            f = nc.psum_tensor if psum else nc.sbuf_tensor
            setattr(H, nm, stk.enter_context(f("s_" + nm, shape, dt)))

    def phase_A(E, which, eng):
        g = s = t = v = a = eng
        if which == "s":
            for nm, src in [("eye", eye_h), ("e7", e7_h), ("e15", e15_h),
                            ("e13", e13_h), ("pool", pool_h), ("ones1", ones_h),
                            ("bq_r", bqr_h), ("bk_r", bkr_h)]:
                ss.inc(s.dma_start(out=getattr(H, nm)[:, :], in_=src[:, :]), 16)
            E.mark("consts_in", ss)

        if which == "g":
            sp.inc(g.memset(H.warm0[0:1, :], 0.0))
            E.selfwait(g, "g", sp)
            sg.inc(g.dma_start(out=cc_w0.ap(), in_=H.warm0[0:1, :]), 16)
            E.selfwait(g, "g", sg)
            scc.inc(g.collective_compute(
                "AllReduce", mybir.AluOpType.max,
                replica_groups=[list(range(8))],
                ins=[cc_w0.ap()], outs=[cc_w1.ap()]))
            for kc in range(8):
                sg.inc(g.dma_start(out=H.wq_sb[:, kc * D:(kc + 1) * D],
                                   in_=wqt_h[kc * 128:(kc + 1) * 128, :]), 16)
            E.mark("wq_in", sg)
            for i in range(4):
                sg.inc(g.dma_start(out=H.xbf[:, (i % 4) * D:(i % 4 + 1) * D],
                                   in_=x_h[i * 128:(i + 1) * 128, :]), 16)
                E.mark(("A_x_in", i), sg)
                E.selfwait(g, "g", sg)
            for i in range(4, NT):
                if i >= 4:
                    E.wait(g, "g", st, key=("A_px_mm", i - 4))
                sg.inc(g.dma_start(out=H.xbf[:, (i % 4) * D:(i % 4 + 1) * D],
                                   in_=x_h[i * 128:(i + 1) * 128, :]), 16)
                E.mark(("A_x_in", i), sg)
                E.selfwait(g, "g", sg)
                if i == 7:
                    for kc in range(8):
                        sg.inc(g.dma_start(
                            out=H.wk_sb[:, kc * D:(kc + 1) * D],
                            in_=wkt_h[kc * 128:(kc + 1) * 128, :]), 16)
                    E.mark("wk_in", sg)
                    E.selfwait(g, "g", sg)

        if which == "v":
            sv.inc(v.memset(H.qlmp[:, :], 0.0))
            sv.inc(v.memset(H.klmp[:, :], 0.0))
            E.mark("pads0", sv)
            E.wait(v, "v", ss, key="consts_in")
            sv.inc(v.tensor_scalar(out=H.bqs[:, :], in0=H.bq_r[:, :],
                                   scalar1=SCALE, scalar2=None, op0=ALU.mult))
            sv.inc(v.tensor_scalar(out=H.bks[:, :], in0=H.bk_r[:, :],
                                   scalar1=SCALE, scalar2=None, op0=ALU.mult))
            E.mark("bqs", sv)
            for i in range(NT):
                srp, j = i // 4, i % 4
                E.wait(v, "v", st, key=("A_xt_mm", i))
                if j == 0 and srp >= 1:
                    E.wait(v, "v", st, key=("A_qk_mm", srp - 1, 1, 7))
                dst = _r3(H.xts, 8)
                sv.inc(v.tensor_copy(dst[:, :, j * 128:(j + 1) * 128]
                                     if not E.dry else dst,
                                     _r3(getattr(H, f"xtps{i % 2}"), 8)))
                E.mark(("A_xt_ev", i), sv)
                E.wait(v, "v", st, key=("A_px_mm", i))
                pxd = _r3(H.px, 8)
                sv.inc(v.tensor_copy(pxd[:, :, 4 * i % 128:4 * i % 128 + 4]
                                     if not E.dry else pxd,
                                     H.pxps[:, :]))
                E.mark(("A_px_ev", i), sv)

        if which == "t":
            E.wait(t, "t", ss, key="consts_in")
            qkbank = [0]
            for i in range(NT):
                srp, j = i // 4, i % 4
                E.wait(t, "t", sg, key=("A_x_in", i))
                if i >= 2:
                    E.wait(t, "t", sv, key=("A_xt_ev", i - 2))
                xb = H.xbf[:, (i % 4) * D:(i % 4 + 1) * D]
                xtp = getattr(H, f"xtps{i % 2}")
                for c in range(8):
                    mm = t.transpose(xtp[:, c * 128:(c + 1) * 128],
                                     xb[:, c * 128:(c + 1) * 128], H.eye[:, :])
                    if c == 7:
                        st.inc(mm)
                E.mark(("A_xt_mm", i), st)
                if i >= 1:
                    E.wait(t, "t", sv, key=("A_px_ev", i - 1))
                for c in range(8):
                    mm = t.matmul(H.pxps[:, c * 4:(c + 1) * 4],
                                  xb[:, c * 128:(c + 1) * 128], H.pool[:, :],
                                  start=True, stop=True, skip_group_check=True)
                    if c == 7:
                        st.inc(mm)
                E.mark(("A_px_mm", i), st)
                if j == 3:
                    E.wait(t, "t", sv, key=("A_xt_ev", i))
                    for ip, wsb in enumerate([H.wq_sb, H.wk_sb]):
                        E.wait(t, "t", sg,
                               key="wq_in" if ip == 0 else "wk_in")
                        for mc in range(8):
                            bank = getattr(H, f"qkps{qkbank[0] % 2}")
                            nq = qkbank[0]
                            if nq >= 2:
                                E.wait(t, "t", sa, val=E.M.get(("A_qk_ev_n", nq - 2)))
                            for kc in range(8):
                                mm = t.matmul(
                                    bank[:, :],
                                    wsb[:, kc * D + mc * 128:kc * D + (mc + 1) * 128],
                                    H.xts[:, kc * 512:(kc + 1) * 512],
                                    start=(kc == 0), stop=(kc == 7))
                                if kc == 7:
                                    st.inc(mm)
                            E.mark(("A_qk_mm", srp, ip, mc), st)
                            qkbank[0] += 1
            E.wait(t, "t", sv, key=("A_px_ev", NT - 1))
            for ip, wsb in enumerate([H.wq_sb, H.wk_sb]):
                for mc in range(8):
                    bank = getattr(H, f"qkps{qkbank[0] % 2}")
                    nq = qkbank[0]
                    E.wait(t, "t", sa, val=E.M.get(("A_qk_ev_n", nq - 2)))
                    for kc in range(8):
                        mm = t.matmul(bank[:, 0:128],
                                      wsb[:, kc * D + mc * 128:kc * D + (mc + 1) * 128],
                                      H.px[:, kc * 128:(kc + 1) * 128],
                                      start=(kc == 0), stop=(kc == 7))
                        if kc == 7:
                            st.inc(mm)
                    E.mark(("A_lm_mm", ip, mc), st)
                    qkbank[0] += 1

        if which == "a":
            E.wait(a, "a", sv, key="bqs")
            nq = 0
            for srp in range(NSTRIP):
                for ip, (dst, bias) in enumerate([(H.qt, H.bqs), (H.kt, H.bks)]):
                    for mc in range(8):
                        bank = getattr(H, f"qkps{nq % 2}")
                        E.wait(a, "a", st, key=("A_qk_mm", srp, ip, mc))
                        sa.inc(a.activation(
                            out=dst[:, mc * L + srp * 512:mc * L + srp * 512 + 512],
                            in_=bank[:, :], func=AF.Identity,
                            bias=bias[:, mc:mc + 1], scale=SCALE))
                        E.mark(("A_qk_ev", srp, ip, mc), sa)
                        E.mark(("A_qk_ev_n", nq), sa)
                        nq += 1
            for ip, (dst, bias) in enumerate([(H.qlm, H.bqs), (H.klm, H.bks)]):
                for mc in range(8):
                    bank = getattr(H, f"qkps{nq % 2}")
                    E.wait(a, "a", st, key=("A_lm_mm", ip, mc))
                    sa.inc(a.activation(
                        out=dst[:, mc * 128:(mc + 1) * 128],
                        in_=bank[:, 0:128], func=AF.Identity,
                        bias=bias[:, mc:mc + 1], scale=SCALE))
                    E.mark(("A_lm_ev", ip, mc), sa)
                    E.mark(("A_qk_ev_n", nq), sa)
                    nq += 1

    def phase_B(E, which, eng, fence):
        g = s = t = v = a = eng

        def sl(i):
            return H.ps8[:, i * 512:(i + 1) * 512]

        def slg(i, off, n):
            return H.ps8[:, i * 512 + off:i * 512 + off + n]

        if which == "v":
            E.fence_wait(v, "v", fence)
            E.wait(v, "v", sv, key="pads0")
            for dst, srcq in [(H.qlmp, H.qlm), (H.klmp, H.klm)]:
                d3 = _r3(dst, 8, (0, 64))
                s3 = _r3(srcq, 8, (0, 64))
                sv.inc(v.tensor_copy(d3[:, :, 0:128] if not E.dry else d3, s3))
                d3 = _r3(dst, 8, (64, 128))
                s3 = _r3(srcq, 8, (64, 128))
                sv.inc(v.tensor_copy(d3[:, :, 128:256] if not E.dry else d3, s3))
            E.mark("pads", sv)
            for w in range(2):
                E.wait(v, "v", st, key=("B_k2_mm", w))
                bank = H.ps8[:, w * 2048:(w + 1) * 2048]
                for hh in range(8):
                    h = 8 * w + hh
                    sv.inc(v.tensor_reduce(H.negmx2[:, h:h + 1],
                                           bank[:, hh * 128:(hh + 1) * 128],
                                           axis=AX.X, op=ALU.max, negate=True))
                E.mark(("B_k2_mx", w), sv)
                E.wait(v, "v", sa, key=("B_k2_exp", w))
                sv.inc(v.reciprocal(H.k2rec[:, 8 * w:8 * w + 8],
                                    H.k2sum[:, 8 * w:8 * w + 8]))
                E.selfwait(v, "v", sv)
                for hh in range(8):
                    h = 8 * w + hh
                    sv.inc(v.tensor_scalar(
                        out=H.k2sb[:, h * 128:(h + 1) * 128],
                        in0=bank[:, hh * 128:(hh + 1) * 128],
                        scalar1=H.k2rec[:, h:h + 1], scalar2=None, op0=ALU.mult))
                E.mark(("B_k2_ev", w), sv)
            E.wait(v, "v", st, key="B_cs_mm")
            sv.inc(v.tensor_reduce(H.lmax[0:1, :], H.ps8[0:1, :2048],
                                   axis=AX.X, op=ALU.max))
            E.mark("lmax", sv)
            for w in range(2):
                E.wait(v, "v", st, key=("B_k2t_mm", w))
                sv.inc(v.tensor_copy(H.k2t[:, w * 1024:(w + 1) * 1024],
                                     H.ps8[:, 2048:3072]))
                E.mark(("B_k2t_ev", w), sv)
            E.wait(v, "v", sg, key="gmax_in")
            sv.inc(v.reciprocal(H.rg[:, :], H.gmax[:, :]))
            E.selfwait(v, "v", sv)
            sv.inc(v.tensor_scalar(out=H.vc0[:, :], in0=H.k2t[:, :],
                                   scalar1=H.rg[:, 0:1], scalar2=None,
                                   op0=ALU.mult))
            sv.inc(v.tensor_scalar(out=H.vt0[:, :], in0=H.k2sb[:, :],
                                   scalar1=H.rg[:, 0:1], scalar2=None,
                                   op0=ALU.mult))
            E.mark("v0", sv)
            for it in range(6):
                for grp in range(4):
                    E.wait(v, "v", st, key=("B_nm", it, grp, "M"))
                    sv.inc(v.scalar_tensor_tensor(
                        out=_r3(H.a1sb, 16)[:, grp * 4:(grp + 1) * 4]
                        if not E.dry else _FAKE,
                        in0=_r3(sl(grp), 4) if not E.dry else _FAKE,
                        scalar=-1.0, in1=_brc(H.e7),
                        op0=ALU.mult, op1=ALU.add))
                    E.mark(("B_a1", it, grp), sv)
                for grp in range(4):
                    E.wait(v, "v", st, key=("B_nm", it, grp, "P1"))
                    sv.inc(v.scalar_tensor_tensor(
                        out=_r3(H.a2sb, 16)[:, grp * 4:(grp + 1) * 4]
                        if not E.dry else _FAKE,
                        in0=_r3(sl(grp), 4) if not E.dry else _FAKE,
                        scalar=-1.0, in1=_brc(H.e15),
                        op0=ALU.mult, op1=ALU.add))
                    E.mark(("B_a2", it, grp), sv)
                for grp in range(4):
                    E.wait(v, "v", st, key=("B_nm", it, grp, "P2"))
                    sv.inc(v.scalar_tensor_tensor(
                        out=_r3(H.a3sb, 16)[:, grp * 4:(grp + 1) * 4]
                        if not E.dry else _FAKE,
                        in0=_r3(sl(4 + grp), 4) if not E.dry else _FAKE,
                        scalar=-1.0, in1=_brc(H.e13),
                        op0=ALU.mult, op1=ALU.add))
                    E.mark(("B_a3", it, grp), sv)

        if which == "a":
            for w in range(2):
                bank = H.ps8[:, w * 2048:(w + 1) * 2048]
                E.wait(a, "a", sv, key=("B_k2_mx", w))
                for hh in range(8):
                    h = 8 * w + hh
                    sa.inc(a.activation(
                        out=bank[:, hh * 128:(hh + 1) * 128],
                        in_=bank[:, hh * 128:(hh + 1) * 128],
                        func=AF.Exp, bias=H.negmx2[:, h:h + 1], scale=1.0,
                        accum_out=H.k2sum[:, h:h + 1]))
                E.mark(("B_k2_exp", w), sa)
            for it in range(6):
                vcn = [H.vc0, H.vc1][(it + 1) % 2]
                vtn = [H.vt0, H.vt1][(it + 1) % 2]
                for grp in range(4):
                    E.wait(a, "a", st, key=("B_nm", it, grp, "MT"))
                    sa.inc(a.activation(out=H.mtsb[:, grp * 512:(grp + 1) * 512],
                                        in_=sl(4 + grp)[:, :],
                                        func=AF.Identity))
                    E.mark(("B_mt", it, grp), sa)
                for grp in range(4):
                    cols = slice(grp * 512, (grp + 1) * 512)
                    E.wait(a, "a", st, key=("B_nm", it, grp, "Vn"))
                    sa.inc(a.activation(out=vcn[:, cols], in_=sl(grp)[:, :],
                                        func=AF.Identity, scale=0.25))
                    E.mark(("B_vn", it, grp), sa)
                for grp in range(4):
                    cols = slice(grp * 512, (grp + 1) * 512)
                    E.wait(a, "a", st, key=("B_nm", it, grp, "VnT"))
                    sa.inc(a.activation(out=vtn[:, cols], in_=sl(4 + grp)[:, :],
                                        func=AF.Identity, scale=0.25))
                    E.mark(("B_vnt", it, grp), sa)

        if which == "t":
            E.fence_wait(t, "t", fence)
            E.wait(t, "t", sv, key="pads")
            for w in range(2):
                bank = H.ps8[:, w * 2048:(w + 1) * 2048]
                for hh in range(8):
                    h = 8 * w + hh
                    c, hf = h // 2, h % 2
                    s2_ = slice(c * 256 + hf * 128, c * 256 + hf * 128 + 128)
                    mm = t.matmul(bank[:, hh * 128:(hh + 1) * 128],
                                  H.qlmp[:, s2_], H.klmp[:, s2_],
                                  start=True, stop=True, skip_group_check=True)
                    if hh == 7:
                        st.inc(mm)
                E.mark(("B_k2_mm", w), st)
            E.wait(t, "t", sv, key=("B_k2_ev", 1))
            mm = None
            for h in range(NH):
                mm = t.matmul(H.ps8[0:1, h * 128:(h + 1) * 128],
                              H.ones1[:, :], H.k2sb[:, h * 128:(h + 1) * 128],
                              start=True, stop=True, skip_group_check=True)
            st.inc(mm)
            E.mark("B_cs_mm", st)
            E.wait(t, "t", sv, key="lmax")
            for w in range(2):
                if w >= 1:
                    E.wait(t, "t", sv, key=("B_k2t_ev", 0))
                for hh in range(8):
                    h = 8 * w + hh
                    mm = t.matmul(H.ps8[:, 2048 + hh * 128:2048 + (hh + 1) * 128],
                                  H.k2sb[:, h * 128:(h + 1) * 128], H.eye[:, :],
                                  start=True, stop=True, skip_group_check=True)
                    if hh == 7:
                        st.inc(mm)
                E.mark(("B_k2t_mm", w), st)
            E.wait(t, "t", sv, key="v0")
            for it in range(6):
                vcc = [H.vc0, H.vc1][it % 2]
                vtc = [H.vt0, H.vt1][it % 2]
                for grp in range(4):
                    if it >= 1:
                        E.wait(t, "t", sa, key=("B_vn", it - 1, grp))
                        E.wait(t, "t", sv, key=("B_a1", it - 1, 3))
                    for hh in range(4):
                        h = grp * 4 + hh
                        mm = t.matmul(slg(grp, hh * 128, 128),
                                      H.k2t[:, h * 128:(h + 1) * 128],
                                      vcc[:, h * 128:(h + 1) * 128],
                                      start=True, stop=True, skip_group_check=True)
                        if hh == 3:
                            st.inc(mm)
                    E.mark(("B_nm", it, grp, "M"), st)
                for grp in range(4):
                    if it >= 1:
                        E.wait(t, "t", sa, key=("B_vnt", it - 1, grp))
                    for hh in range(4):
                        h = grp * 4 + hh
                        mm = t.matmul(slg(4 + grp, hh * 128, 128),
                                      vcc[:, h * 128:(h + 1) * 128],
                                      H.k2t[:, h * 128:(h + 1) * 128],
                                      start=True, stop=True, skip_group_check=True)
                        if hh == 3:
                            st.inc(mm)
                    E.mark(("B_nm", it, grp, "MT"), st)
                for grp in range(4):
                    E.wait(t, "t", sv, key=("B_a1", it, grp))
                    E.wait(t, "t", sa, key=("B_mt", it, grp))
                    for hh in range(4):
                        mm = t.matmul(slg(grp, hh * 128, 128),
                                      H.mtsb[:, grp * 512 + hh * 128:
                                             grp * 512 + (hh + 1) * 128],
                                      H.a1sb[:, grp * 512 + hh * 128:
                                             grp * 512 + (hh + 1) * 128],
                                      start=True, stop=True, skip_group_check=True)
                        if hh == 3:
                            st.inc(mm)
                    E.mark(("B_nm", it, grp, "P1"), st)
                for grp in range(4):
                    E.wait(t, "t", sv, key=("B_a2", it, grp))
                    for hh in range(4):
                        mm = t.matmul(slg(4 + grp, hh * 128, 128),
                                      H.mtsb[:, grp * 512 + hh * 128:
                                             grp * 512 + (hh + 1) * 128],
                                      H.a2sb[:, grp * 512 + hh * 128:
                                             grp * 512 + (hh + 1) * 128],
                                      start=True, stop=True, skip_group_check=True)
                        if hh == 3:
                            st.inc(mm)
                    E.mark(("B_nm", it, grp, "P2"), st)
                for grp in range(4):
                    E.wait(t, "t", sv, key=("B_a3", it, grp))
                    if it >= 1:
                        E.wait(t, "t", sa, key=("B_vnt", it - 1, grp))
                    for hh in range(4):
                        h = grp * 4 + hh
                        mm = t.matmul(slg(grp, hh * 128, 128),
                                      vtc[:, h * 128:(h + 1) * 128],
                                      H.a3sb[:, grp * 512 + hh * 128:
                                             grp * 512 + (hh + 1) * 128],
                                      start=True, stop=True, skip_group_check=True)
                        if hh == 3:
                            st.inc(mm)
                    E.mark(("B_nm", it, grp, "Vn"), st)
                for grp in range(4):
                    for hh in range(4):
                        h = grp * 4 + hh
                        mm = t.matmul(slg(4 + grp, hh * 128, 128),
                                      H.a3sb[:, grp * 512 + hh * 128:
                                             grp * 512 + (hh + 1) * 128],
                                      vtc[:, h * 128:(h + 1) * 128],
                                      start=True, stop=True, skip_group_check=True)
                        if hh == 3:
                            st.inc(mm)
                    E.mark(("B_nm", it, grp, "VnT"), st)

        if which == "g":
            E.wait(g, "g", sv, key="lmax")
            sg.inc(g.dma_start(out=cc_in.ap(), in_=H.lmax[0:1, :]), 16)
            E.selfwait(g, "g", sg)
            scc.inc(g.collective_compute(
                "AllReduce", mybir.AluOpType.max,
                replica_groups=[list(range(8))],
                ins=[cc_in.ap()], outs=[cc_out.ap()]))

    def _brc(eyet):
        if _isfake(eyet):
            return eyet
        return eyet.ap().unsqueeze(1).to_broadcast((128, 4, 128))

    def phase_C(E, which, eng, fence):
        g = s = t = v = a = eng
        LAGC = 3
        if which == "g":
            E.wait(g, "g", st, val=fence["st_A"])
            for i in range(4):
                sg.inc(g.dma_start(out=H.xbf[:, (i % 4) * D:(i % 4 + 1) * D],
                                   in_=x_h[i * 128:(i + 1) * 128, :]), 16)
                E.mark(("C_x_in", i), sg)
                E.selfwait(g, "g", sg)
            E.wait(g, "g", scc, val=2)
            sg.inc(g.dma_start(out=H.gmax[:, :],
                               in_=cc_out.ap().to_broadcast((128, 1))), 16)
            E.mark("gmax_in", sg)
            E.selfwait(g, "g", sg)
            E.wait(g, "g", st, val=fence["st"])
            for kc in range(8):
                sg.inc(g.dma_start(out=H.wv_sb[:, kc * D:(kc + 1) * D],
                                   in_=wvt_h[kc * 128:(kc + 1) * 128, :]), 16)
            E.mark("wv_in", sg)
            E.selfwait(g, "g", sg)
            for i in range(4, NT):
                if i >= 4:
                    E.wait(g, "g", st, key=("C_xt_mm", i - 4))
                sg.inc(g.dma_start(out=H.xbf[:, (i % 4) * D:(i % 4 + 1) * D],
                                   in_=x_h[i * 128:(i + 1) * 128, :]), 16)
                E.mark(("C_x_in", i), sg)
                E.selfwait(g, "g", sg)

        if which == "s":
            E.fence_wait(s, "s", fence)
            ss.inc(s.dma_start(out=H.bvb_sb[:, :], in_=bvb_h[:, :]), 16)
            E.mark("bvb_in", ss)

        if which == "t":
            E.fence_wait(t, "t", fence)
            E.wait(t, "t", ss, key="bvb_in")
            E.wait(t, "t", sg, key="wv_in")
            for p in range(NT + LAGC):
                if 0 <= p < NT:
                    E.wait(t, "t", sg, key=("C_x_in", p))
                    if p >= 2:
                        E.wait(t, "t", sv, key=("C_xt_ev", p - 2))
                    xb = H.xbf[:, (p % 4) * D:(p % 4 + 1) * D]
                    xtp = getattr(H, f"xtps{p % 2}")
                    for c in range(8):
                        mm = t.transpose(xtp[:, c * 128:(c + 1) * 128],
                                         xb[:, c * 128:(c + 1) * 128], H.eye[:, :])
                        if c == 7:
                            st.inc(mm)
                    E.mark(("C_xt_mm", p), st)
                i = p - 1
                if 0 <= i < NT:
                    xt_t = H.xts[:, (i % 2) * 1024:(i % 2 + 1) * 1024]
                    E.wait(t, "t", sv, key=("C_xt_ev", i))
                    if i >= 1:
                        E.wait(t, "t", sv, key=("C_v_ev", i - 1, 1))
                    for kc in range(8):
                        mm = t.matmul(
                            H.vps[:, :], xt_t[:, kc * 128:(kc + 1) * 128],
                            H.wv_sb[:, kc * D:kc * D + 512],
                            start=(kc == 0), stop=(kc == 7))
                        if kc == 7:
                            st.inc(mm)
                    E.mark(("C_v_mm", i, 0), st)
                    if i >= 1:
                        E.wait(t, "t", sa, key=("C_k3_ev", i - 1, 1))
                    for cp in range(4):
                        mm = t.matmul(H.k3ps[:, cp * 256:(cp + 1) * 256],
                                      H.kt[:, cp * L + i * 128:cp * L + i * 128 + 128],
                                      H.qlmp[:, cp * 256:(cp + 1) * 256],
                                      start=True, stop=True, skip_group_check=True)
                        if cp == 3:
                            st.inc(mm)
                    E.mark(("C_k3_mm", i, 0), st)
                    E.wait(t, "t", sv, key=("C_v_ev", i, 0))
                    for kc in range(8):
                        mm = t.matmul(
                            H.vps[:, :], xt_t[:, kc * 128:(kc + 1) * 128],
                            H.wv_sb[:, kc * D + 512:kc * D + 1024],
                            start=(kc == 0), stop=(kc == 7))
                        if kc == 7:
                            st.inc(mm)
                    E.mark(("C_v_mm", i, 1), st)
                    E.wait(t, "t", sa, key=("C_k3_ev", i, 0))
                    for cp in range(4):
                        c = 4 + cp
                        mm = t.matmul(H.k3ps[:, cp * 256:(cp + 1) * 256],
                                      H.kt[:, c * L + i * 128:c * L + i * 128 + 128],
                                      H.qlmp[:, c * 256:(c + 1) * 256],
                                      start=True, stop=True, skip_group_check=True)
                        if cp == 3:
                            st.inc(mm)
                    E.mark(("C_k3_mm", i, 1), st)
                i = p - 2
                if 0 <= i < NT:
                    for w in range(2):
                        E.wait(t, "t", sa, key=("C_k3_ev", i, w))
                        E.wait(t, "t", sv, key=("C_v_ev", i, 1))
                        mm2 = None
                        for hh in range(8):
                            h = 8 * w + hh
                            k3sl = H.k3e[:, (i % 2) * 2048 + h * 128:
                                         (i % 2) * 2048 + h * 128 + 128]
                            bbank = getattr(H, f"bps{h // 8}")
                            t.matmul(bbank[:, (h % 8) * 64:(h % 8) * 64 + 64],
                                     k3sl, H.vsb[:, (i % 2) * 1024 + h * 64:
                                                 (i % 2) * 1024 + h * 64 + 64],
                                     start=(i == 0 and h % 8 == 0),
                                     stop=(i == NT - 1),
                                     skip_group_check=True)
                            mm2 = t.matmul(H.dps[:, h:h + 1], k3sl, H.ones1[:, :],
                                           start=(i == 0 and h == 0),
                                           stop=(i == NT - 1),
                                           skip_group_check=True)
                        st.inc(mm2)
                        E.mark(("C_b_mm", i, w), st)
            E.wait(t, "t", sv, key="bnorm")
            for w in range(2):
                if w:
                    E.wait(t, "t", sv, key=("C_w2_ev", 0))
                mm = None
                for hh in range(8):
                    h = 8 * w + hh
                    mm = t.matmul(H.k3ps[:, hh * 64:(hh + 1) * 64],
                                  H.vt0[:, h * 128:(h + 1) * 128],
                                  H.bnorm[:, h * 64:(h + 1) * 64],
                                  start=(hh == 0), stop=True,
                                  skip_group_check=True)
                st.inc(mm)
                E.mark(("C_w2_mm", w), st)

        if which == "a":
            for p in range(NT + LAGC):
                i = p - 1
                if 0 <= i < NT:
                    for w in range(2):
                        E.wait(a, "a", st, key=("C_k3_mm", i, w))
                        if i >= 2:
                            E.wait(a, "a", st, key=("C_b_mm", i - 2, w))
                        sa.inc(a.activation(
                            out=H.k3e[:, (i % 2) * 2048 + w * 1024:
                                      (i % 2) * 2048 + w * 1024 + 1024],
                            in_=H.k3ps[:, :], func=AF.Exp))
                        E.mark(("C_k3_ev", i, w), sa)

        if which == "v":
            E.fence_wait(v, "v", fence)
            for p in range(NT + LAGC):
                if 0 <= p < NT:
                    E.wait(v, "v", st, key=("C_xt_mm", p))
                    if p >= 2:
                        E.wait(v, "v", st, key=("C_v_mm", p - 2, 1))
                    sv.inc(v.tensor_copy(H.xts[:, (p % 2) * 1024:(p % 2 + 1) * 1024],
                                         getattr(H, f"xtps{p % 2}")[:, :]))
                    E.mark(("C_xt_ev", p), sv)
                i = p - 1
                if 0 <= i < NT:
                    for half in range(2):
                        E.wait(v, "v", st, key=("C_v_mm", i, half))
                        if i >= 2 and half == 0:
                            E.wait(v, "v", st, key=("C_b_mm", i - 2, 1))
                        sv.inc(v.scalar_tensor_tensor(
                            out=H.vsb[:, (i % 2) * 1024 + half * 512:
                                      (i % 2) * 1024 + half * 512 + 512],
                            in0=H.vps[:, :], scalar=1.0,
                            in1=H.bvb_sb[:, half * 512:half * 512 + 512],
                            op0=ALU.mult, op1=ALU.add))
                        E.mark(("C_v_ev", i, half), sv)
            E.wait(v, "v", st, key=("C_b_mm", NT - 1, 1))
            sv.inc(v.tensor_copy(H.d_sb[:, :], H.dps[:, :]))
            E.selfwait(v, "v", sv)
            sv.inc(v.reciprocal(H.recd[:, :], H.d_sb[:, :]))
            E.selfwait(v, "v", sv)
            for h in range(NH):
                bbank = getattr(H, f"bps{h // 8}")
                sv.inc(v.tensor_scalar(
                    out=H.bnorm[:, h * 64:(h + 1) * 64],
                    in0=bbank[:, (h % 8) * 64:(h % 8) * 64 + 64],
                    scalar1=H.recd[:, h:h + 1], scalar2=None, op0=ALU.mult))
            E.mark("bnorm", sv)
            for w in range(2):
                E.wait(v, "v", st, key=("C_w2_mm", w))
                dst = _r3(H.w2aug, NH)
                sv.inc(v.tensor_copy(
                    dst[:, w * 8:(w + 1) * 8, 0:64] if not E.dry else dst,
                    _r3(H.k3ps, 16)[:, 0:8, :] if not E.dry else _FAKE))
                E.mark(("C_w2_ev", w), sv)
            dst = _r3(H.w2aug, NH)
            sv.inc(v.memset(dst[:, :, 64:65] if not E.dry else dst, 1.0))
            E.mark("w2ones", sv)

    def phase_D(E, which, eng, fence):
        g = s = t = v = a = eng
        LAG = 8
        G3 = [(0, 6), (6, 12), (12, 16)]

        if which == "s":
            E.fence_wait(s, "s", fence)
            ss.inc(s.dma_start(out=H.d2_sb[:, :], in_=d2_h[:, :]), 16)
            E.mark("d2_in", ss)
            E.selfwait(s, "s", ss)
            for kc in range(8):
                if kc >= 1:
                    E.wait(s, "s", sv, key=("D_wft", kc - 1))
                ss.inc(s.dma_start(out=H.wtmp[:, :],
                                   in_=wft_h[kc * 128:(kc + 1) * 128, :]), 16)
                E.mark(("D_wft_in", kc), ss)
                E.selfwait(s, "s", ss)
            for p in range(NT + LAG):
                i = p - 6
                if 0 <= i < NT:
                    E.wait(s, "s", sp, key=("D_out", i))
                    so.inc(s.dma_start(
                        out=out_h[i * 128:(i + 1) * 128, :],
                        in_=H.outst[:, (i % 2) * D:(i % 2 + 1) * D]), 16)
                    E.selfwait(s, "s", so)
            if not E.dry:
                s.wait_ge(so.sem, NT * 16)

        if which == "g":
            E.fence_wait(g, "g", fence)
            for i0 in range(2):
                sg.inc(g.dma_start(out=H.x32[:, (i0 % 2) * D:(i0 % 2 + 1) * D],
                                   in_=x_h[i0 * 128:(i0 + 1) * 128, :]), 16)
                E.mark(("D_x_in", i0), sg)
                E.selfwait(g, "g", sg)
            for p in range(NT + LAG):
                i = p - 2
                if 0 <= i < NT:
                    E.wait(g, "g", sa, key=("D_rstd1", i))
                    if i >= 3:
                        E.wait(g, "g", sv, key=("D_r2h", i - 3, 1))
                    E.selfwait(g, "g", sp)
                    sp.inc(g.tensor_scalar(
                        out=H.y32[:, (i % 3) * D:(i % 3 + 1) * D],
                        in0=H.r_sb[:, (i % 3) * D:(i % 3 + 1) * D],
                        scalar1=H.negm1[:, i % 2:i % 2 + 1],
                        scalar2=H.rstd1[:, i % 2:i % 2 + 1],
                        op0=ALU.add, op1=ALU.mult))
                    E.mark(("D_y32", i), sp)
                if 0 <= p < NT:
                    E.wait(g, "g", sv, key=("D_xn", p))
                    E.wait(g, "g", sg, key=("D_x_in", p))
                    E.selfwait(g, "g", sp)
                    sp.inc(g.tensor_tensor(
                        out=H.r_sb[:, (p % 3) * D:(p % 3 + 1) * D],
                        in0=H.xn[:, (p % 2) * D:(p % 2 + 1) * D],
                        in1=H.x32[:, (p % 2) * D:(p % 2 + 1) * D], op=ALU.add))
                    E.mark(("D_r", p), sp)
                i2 = p + 2
                if 2 <= i2 < NT:
                    E.wait(g, "g", sp, key=("D_r", i2 - 2))
                    sg.inc(g.dma_start(out=H.x32[:, (i2 % 2) * D:(i2 % 2 + 1) * D],
                                       in_=x_h[i2 * 128:(i2 + 1) * 128, :]), 16)
                    E.mark(("D_x_in", i2), sg)
                    E.selfwait(g, "g", sg)
                i = p - 5
                if 0 <= i < NT:
                    E.wait(g, "g", sa, key=("D_rstd2", i))
                    if i >= 2:
                        E.wait(g, "g", so, val=16 * (i - 1))
                    E.selfwait(g, "g", sp)
                    sp.inc(g.tensor_scalar(
                        out=H.outst[:, (i % 2) * D:(i % 2 + 1) * D],
                        in0=H.r2_sb[:, (i % 2) * D:(i % 2 + 1) * D],
                        scalar1=H.negm2[:, i % 2:i % 2 + 1],
                        scalar2=H.rstd2[:, i % 2:i % 2 + 1],
                        op0=ALU.add, op1=ALU.mult))
                    E.mark(("D_out", i), sp)

        if which == "t":
            E.wait(t, "t", sv, key=("C_xt_ev", NT - 1))

            def emit_k1h(s_, h):
                nb = s_ * 16 + h
                if nb >= 2:
                    pn = nb - 2
                    E.wait(t, "t", sa, key=("D_k1_ev", pn // 16, pn % 16))
                c, hf = h // 2, h % 2
                ksl = slice(c * 256 + hf * 128, c * 256 + hf * 128 + 128)
                bank = [H.k1psA, H.k1psB][nb % 2]
                mm = t.matmul(bank[:, :], H.klmp[:, ksl],
                              H.qt[:, c * L + s_ * 512:c * L + s_ * 512 + 512],
                              start=True, stop=True, skip_group_check=True)
                st.inc(mm)
                E.mark(("D_k1_mm", s_, h), st)

            def emit_x(p, g3):
                h0, h1 = G3[g3]
                s_, j = p // 4, p % 4
                E.wait(t, "t", sa, key=("D_k1_ev", s_, h1 - 1))
                if p >= 1:
                    E.wait(t, "t", sv, key=("D_xn", p - 1, g3))
                xqt = getattr(H, f"xq{g3}")
                hs = 2 * s_ + (j // 2)
                mm = None
                for idx, hh in enumerate(range(h0, h1)):
                    k1sl = H.k1e[:, (hs % 3) * 4096 + hh * 256 + (j % 2) * 128:
                                 (hs % 3) * 4096 + hh * 256 + (j % 2) * 128 + 128]
                    mm = t.matmul(xqt[:, idx * 65:idx * 65 + 65],
                                  k1sl, H.w2aug[:, hh * 65:hh * 65 + 65],
                                  start=(idx == 0), stop=True,
                                  skip_group_check=True)
                st.inc(mm)
                E.mark(("D_x_mm", p, g3), st)

            for h in range(16):
                emit_k1h(0, h)
            for p in range(NT + LAG):
                s_, j = p // 4, p % 4
                if 0 <= p < NT:
                    E.wait(t, "t", sv, key="w2ones")
                    for g3 in range(3):
                        if s_ + 1 < NSTRIP:
                            emit_k1h(s_ + 1, 4 * j + g3)
                        emit_x(p, g3)
                    if s_ + 1 < NSTRIP:
                        emit_k1h(s_ + 1, 4 * j + 3)
                i = p - 3
                if 0 <= i < NT:
                    E.wait(t, "t", sv, key=("D_y16", i))
                    if i >= 1:
                        E.wait(t, "t", sv, key=("D_yt_ev", i - 1))
                    for c in range(8):
                        mm = t.transpose(H.ytps[:, c * 128:(c + 1) * 128],
                                         H.y16[:, (i % 2) * D + c * 128:
                                               (i % 2) * D + (c + 1) * 128],
                                         H.eye[:, :])
                        if c == 7:
                            st.inc(mm)
                    E.mark(("D_yt_mm", i), st)
                i = p - 4
                if 0 <= i < NT:
                    E.wait(t, "t", sv, key=("D_yt_ev", i))
                    if i >= 1:
                        E.wait(t, "t", sv, key=("D_r2h", i - 1, 1))
                    for half in range(2):
                        for kc in range(8):
                            mm = t.matmul(
                                H.zps[:, half * 512:half * 512 + 512],
                                H.yt_sb[:, (i % 2) * D + kc * 128:
                                        (i % 2) * D + (kc + 1) * 128],
                                H.wft_sb[:, kc * D + half * 512:
                                         kc * D + half * 512 + 512],
                                start=(kc == 0), stop=(kc == 7))
                            if kc == 7:
                                st.inc(mm)
                        E.mark(("D_z_mm", i, half), st)

        if which == "a":
            E.wait(a, "a", st, key=("C_k3_mm", NT - 1, 1))

            def emit_k1e(s_, h):
                E.wait(a, "a", st, key=("D_k1_mm", s_, h))
                bank = [H.k1psA, H.k1psB][(s_ * 16 + h) % 2]
                for half in range(2):
                    b3 = (2 * s_ + half) % 3
                    sa.inc(a.activation(
                        out=H.k1e[:, b3 * 4096 + h * 256:
                                  b3 * 4096 + h * 256 + 256],
                        in_=bank[:, half * 256:half * 256 + 256], func=AF.Exp))
                E.mark(("D_k1_ev", s_, h), sa)

            for h in range(16):
                emit_k1e(0, h)
            for p in range(NT + LAG):
                s_, j = p // 4, p % 4
                if 0 <= p < NT and s_ + 1 < NSTRIP:
                    for g3 in range(4):
                        emit_k1e(s_ + 1, 4 * j + g3)
                i = p - 1
                if 0 <= i < NT:
                    E.wait(a, "a", sv, key=("D_st1", i))
                    E.selfwait(a, "a", sa)
                    sa.inc(a.activation(out=H.lnv[:, i % 2:i % 2 + 1],
                                        in_=H.veps1[:, i % 2:i % 2 + 1],
                                        func=AF.Ln))
                    E.selfwait(a, "a", sa)
                    sa.inc(a.activation(out=H.rstd1[:, i % 2:i % 2 + 1],
                                        in_=H.lnv[:, i % 2:i % 2 + 1],
                                        func=AF.Exp, scale=-0.5))
                    E.mark(("D_rstd1", i), sa)
                i = p - 4
                if 0 <= i < NT:
                    E.wait(a, "a", sv, key=("D_r2", i))
                    E.selfwait(a, "a", sa)
                    sa.inc(a.activation(out=H.sq_sb[:, :],
                                        in_=H.r2_sb[:, (i % 2) * D:(i % 2 + 1) * D],
                                        func=AF.Square,
                                        accum_out=H.ssq[:, i % 2:i % 2 + 1]))
                    E.mark(("D_ssq", i), sa)
                i = p - 5
                if 0 <= i < NT:
                    E.wait(a, "a", sv, key=("D_st2", i))
                    E.selfwait(a, "a", sa)
                    sa.inc(a.activation(out=H.lnv[:, i % 2:i % 2 + 1],
                                        in_=H.veps2[:, i % 2:i % 2 + 1],
                                        func=AF.Ln))
                    E.selfwait(a, "a", sa)
                    sa.inc(a.activation(out=H.rstd2[:, i % 2:i % 2 + 1],
                                        in_=H.lnv[:, i % 2:i % 2 + 1],
                                        func=AF.Exp, scale=-0.5))
                    E.mark(("D_rstd2", i), sa)

        if which == "v":
            E.fence_wait(v, "v", fence)
            E.wait(v, "v", ss, key="d2_in")
            for kc in range(8):
                E.wait(v, "v", ss, key=("D_wft_in", kc))
                sv.inc(v.tensor_tensor(out=H.wft_sb[:, kc * D:(kc + 1) * D],
                                       in0=H.wtmp[:, :],
                                       in1=H.d2_sb[:, :], op=ALU.add))
                E.mark(("D_wft", kc), sv)
            for p in range(NT + LAG):
                if 0 <= p < NT:
                    for g3, (h0, h1) in enumerate(G3):
                        ng = h1 - h0
                        E.wait(v, "v", st, key=("D_x_mm", p, g3))
                        xqt = getattr(H, f"xq{g3}")
                        rsl = H.recden[:, (p % 2) * 32 + h0:
                                       (p % 2) * 32 + h1]
                        sv.inc(v.reciprocal(
                            rsl.unsqueeze(-1) if not E.dry else _FAKE,
                            _r3(xqt, ng)[:, :, 64:65] if not E.dry else _FAKE))
                        E.selfwait(v, "v", sv)
                        sv.inc(v.tensor_tensor(
                            out=(H.xn[:, (p % 2) * D + h0 * 64:
                                      (p % 2) * D + h1 * 64]
                                 .rearrange("p (h n) -> p h n", h=ng))
                            if not E.dry else _FAKE,
                            in0=_r3(xqt, ng)[:, :, 0:64] if not E.dry else _FAKE,
                            in1=(rsl.unsqueeze(-1).to_broadcast((128, ng, 64)))
                            if not E.dry else _FAKE,
                            op=ALU.mult))
                        E.mark(("D_xn", p, g3), sv)
                    E.mark(("D_xn", p), sv)
                i = p - 2
                if 0 <= i < NT:
                    E.wait(v, "v", sa, key=("D_rstd1", i))
                    sv.inc(v.tensor_scalar(
                        out=H.y16[:, (i % 2) * D:(i % 2 + 1) * D],
                        in0=H.r_sb[:, (i % 3) * D:(i % 3 + 1) * D],
                        scalar1=H.negm1[:, i % 2:i % 2 + 1],
                        scalar2=H.rstd1[:, i % 2:i % 2 + 1],
                        op0=ALU.add, op1=ALU.mult))
                    E.mark(("D_y16", i), sv)
                i = p - 1
                if 0 <= i < NT:
                    E.wait(v, "v", sp, key=("D_r", i))
                    v.bn_stats(H.stats[:, (i % 2) * 12:(i % 2) * 12 + 6],
                               H.r_sb[:, (i % 3) * D:(i % 3) * D + 512])
                    sv.inc(v.bn_stats(H.stats[:, (i % 2) * 12 + 6:(i % 2) * 12 + 12],
                                      H.r_sb[:, (i % 3) * D + 512:(i % 3 + 1) * D]))
                    E.selfwait(v, "v", sv)
                    sv.inc(v.bn_aggr(H.mv1[:, (i % 2) * 2:(i % 2) * 2 + 2],
                                     H.stats[:, (i % 2) * 12:(i % 2 + 1) * 12]))
                    E.selfwait(v, "v", sv)
                    v.tensor_scalar(out=H.negm1[:, i % 2:i % 2 + 1],
                                    in0=H.mv1[:, (i % 2) * 2:(i % 2) * 2 + 1],
                                    scalar1=-1.0, scalar2=None, op0=ALU.mult)
                    sv.inc(v.tensor_scalar(
                        out=H.veps1[:, i % 2:i % 2 + 1],
                        in0=H.mv1[:, (i % 2) * 2 + 1:(i % 2) * 2 + 2],
                        scalar1=EPS, scalar2=None, op0=ALU.add))
                    E.mark(("D_st1", i), sv)
                i = p - 3
                if 0 <= i < NT:
                    E.wait(v, "v", st, key=("D_yt_mm", i))
                    sv.inc(v.tensor_copy(H.yt_sb[:, (i % 2) * D:(i % 2 + 1) * D],
                                         H.ytps[:, :]))
                    E.mark(("D_yt_ev", i), sv)
                i = p - 4
                if 0 <= i < NT:
                    E.wait(v, "v", sp, key=("D_y32", i))
                    if i >= 2:
                        E.wait(v, "v", sp, key=("D_out", i - 2))
                    for half in range(2):
                        E.wait(v, "v", st, key=("D_z_mm", i, half))
                        sv.inc(v.scalar_tensor_tensor(
                            out=H.r2_sb[:, (i % 2) * D + half * 512:
                                        (i % 2) * D + half * 512 + 512],
                            in0=H.zps[:, half * 512:half * 512 + 512],
                            scalar=1.0,
                            in1=H.y32[:, (i % 3) * D + half * 512:
                                      (i % 3) * D + half * 512 + 512],
                            op0=ALU.mult, op1=ALU.add,
                            accum_out=H.s2[:, (i % 2) * 2 + half:
                                           (i % 2) * 2 + half + 1]))
                        E.mark(("D_r2h", i, half), sv)
                    E.mark(("D_r2", i), sv)
                i = p - 5
                if 0 <= i < NT:
                    q = i % 2
                    E.wait(v, "v", sa, key=("D_ssq", i))
                    sv.inc(v.tensor_tensor(out=H.s2t[:, q:q + 1],
                                           in0=H.s2[:, q * 2:q * 2 + 1],
                                           in1=H.s2[:, q * 2 + 1:q * 2 + 2],
                                           op=ALU.add))
                    E.selfwait(v, "v", sv)
                    sv.inc(v.tensor_scalar(out=H.m2[:, q:q + 1],
                                           in0=H.s2t[:, q:q + 1],
                                           scalar1=R1024, scalar2=None,
                                           op0=ALU.mult))
                    E.selfwait(v, "v", sv)
                    v.tensor_scalar(out=H.negm2[:, q:q + 1], in0=H.m2[:, q:q + 1],
                                    scalar1=-1.0, scalar2=None, op0=ALU.mult)
                    v.tensor_tensor(out=H.m2sq[:, q:q + 1], in0=H.m2[:, q:q + 1],
                                    in1=H.m2[:, q:q + 1], op=ALU.mult)
                    sv.inc(v.tensor_scalar(out=H.e2[:, q:q + 1],
                                           in0=H.ssq[:, q:q + 1],
                                           scalar1=R1024, scalar2=None,
                                           op0=ALU.mult))
                    E.selfwait(v, "v", sv)
                    sv.inc(v.scalar_tensor_tensor(out=H.veps2[:, q:q + 1],
                                                  in0=H.m2sq[:, q:q + 1],
                                                  scalar=-1.0,
                                                  in1=H.e2[:, q:q + 1],
                                                  op0=ALU.mult, op1=ALU.add))
                    E.selfwait(v, "v", sv)
                    sv.inc(v.tensor_scalar(out=H.veps2[:, q:q + 1],
                                           in0=H.veps2[:, q:q + 1],
                                           scalar1=EPS, scalar2=None,
                                           op0=ALU.add))
                    E.mark(("D_st2", i), sv)

    ORDER = ["g", "s", "t", "v", "a"]

    def dry_run():
        for c in ctrs.values():
            c.n = 0
        E = Em(ctrs, M, dry=True)
        for w in ORDER:
            phase_A(E, w, _FAKE)
        fA = E.fence_capture()
        fA["st_A"] = fA["st"]
        for w in ORDER:
            phase_B(E, w, _FAKE, fA)
        fB = E.fence_capture()
        fB["st_A"] = fA["st"]
        for w in ORDER:
            phase_C(E, w, _FAKE, fB)
        fC = E.fence_capture()
        for w in ORDER:
            phase_D(E, w, _FAKE, fC)
        return fA, fB, fC

    fA, fB, fC = dry_run()

    for c in ctrs.values():
        c.n = 0
    E = Em(ctrs, M, dry=False)

    def emit_block(fn, *extra):
        with nc.Block() as blk:
            @blk.gpsimd
            def _(g):
                fn(E, "g", g, *extra)

            @blk.sync
            def _(s):
                fn(E, "s", s, *extra)

            @blk.tensor
            def _(t):
                fn(E, "t", t, *extra)

            @blk.vector
            def _(v):
                fn(E, "v", v, *extra)

            @blk.scalar
            def _(a):
                fn(E, "a", a, *extra)

    with es:
        alloc(es, OUTER)
        ctx1 = ExitStack()
        alloc(ctx1, CTX1)
        stA = ExitStack()
        alloc(stA, CTX2)
        alloc(stA, PS_A, psum=True)
        stA.close()
        stB = ExitStack()
        alloc(stB, CTXB)
        alloc(stB, PS_B, psum=True)
        stB.close()
        stC = ExitStack()
        alloc(stC, CTXC)
        alloc(stC, [("xtps0_c", [128, 1024], BF16),
                    ("xtps1_c", [128, 1024], BF16)], psum=True)
        alloc(stC, PS_C, psum=True)
        stC.close()
        ctx1.close()
        stD = ExitStack()
        alloc(stD, CTXD)
        alloc(stD, PS_D, psum=True)
        stD.close()

        xtps_a = (H.xtps0, H.xtps1)

        def run_all(E2, which, eng):
            H.xtps0, H.xtps1 = xtps_a
            phase_A(E2, which, eng)
            phase_B(E2, which, eng, fA)
            H.xtps0, H.xtps1 = H.xtps0_c, H.xtps1_c
            phase_C(E2, which, eng, fB)
            phase_D(E2, which, eng, fC)

        with nc.Block() as blk:
            @blk.gpsimd
            def _(g):
                run_all(E, "g", g)

            @blk.sync
            def _(s):
                run_all(E, "s", s)

            @blk.tensor
            def _(t):
                run_all(E, "t", t)

            @blk.vector
            def _(v):
                run_all(E, "v", v)

            @blk.scalar
            def _(a):
                run_all(E, "a", a)
    return nc


_NC_CACHE = {}


def kernel(L0_attn_x, L0_attn_Wq, L0_attn_bq, L0_attn_Wk, L0_attn_bk,
           L0_attn_Wv, L0_attn_bv, L0_ffn_weight, L0_ffn_disha):
    x = np.ascontiguousarray(np.asarray(L0_attn_x, dtype=np.float32))
    B = x.shape[0]
    wqt = np.ascontiguousarray(np.asarray(L0_attn_Wq, np.float32).T)
    wkt = np.ascontiguousarray(np.asarray(L0_attn_Wk, np.float32).T)
    wvt = np.ascontiguousarray(np.asarray(L0_attn_Wv, np.float32).T)
    wft = np.ascontiguousarray(np.asarray(L0_ffn_weight, np.float32).T)
    bq = np.asarray(L0_attn_bq, np.float32)
    bk = np.asarray(L0_attn_bk, np.float32)
    bv = np.asarray(L0_attn_bv, np.float32)
    dsh = np.asarray(L0_ffn_disha, np.float32)

    bqr = np.ascontiguousarray(bq.reshape(8, 128).T)
    bkr = np.ascontiguousarray(bk.reshape(8, 128).T)
    bvb = np.ascontiguousarray(np.broadcast_to(bv, (128, D)))
    d2 = np.ascontiguousarray(np.vstack([dsh, dsh]))
    eye = np.eye(128, dtype=ml_dtypes.bfloat16)
    e7 = (7.0 * np.eye(128)).astype(ml_dtypes.bfloat16)
    e15 = (15.0 * np.eye(128)).astype(ml_dtypes.bfloat16)
    e13 = (13.0 * np.eye(128)).astype(ml_dtypes.bfloat16)
    poolm = np.zeros((128, 4), dtype=ml_dtypes.bfloat16)
    for tt in range(128):
        poolm[tt, tt // 32] = 1.0 / 32.0
    ones1 = np.ones((128, 1), dtype=ml_dtypes.bfloat16)

    if "nc" not in _NC_CACHE:
        _NC_CACHE["nc"] = build_program()
    nc = _NC_CACHE["nc"]

    shared = dict(wqt=wqt, wkt=wkt, wvt=wvt, wft=wft, bqr=bqr, bkr=bkr,
                  bvb=bvb, d2=d2, eye=eye, e7=e7, e15=e15, e13=e13,
                  pool=poolm, ones1=ones1)
    in_maps = [dict(shared, x=np.ascontiguousarray(x[c])) for c in range(B)]
    res = run_bass_kernel_spmd(nc, in_maps, core_ids=list(range(B)))
    out = np.stack([np.asarray(res.results[c]["out"], np.float32)
                    for c in range(B)], axis=0)
    return out


# version 25
# speedup vs baseline: 1.0169x; 1.0169x over previous
import types
import numpy as np
import ml_dtypes
from contextlib import ExitStack

import concourse.bass as bass
import concourse.mybir as mybir
from concourse.bass_utils import run_bass_kernel_spmd
from concourse.alu_op_type import AluOpType as ALU

F32 = mybir.dt.float32
BF16 = mybir.dt.bfloat16
AF = mybir.ActivationFunctionType
AX = mybir.AxisListType

L = 4096
D = 1024
NH = 16
NT = 32
NSTRIP = 8
SCALE = float(1.0 / np.sqrt(np.sqrt(64.0)))
EPS = 1e-5
R1024 = 1.0 / 1024.0


class _Fake:
    def __getattr__(self, k):
        return lambda *a, **kw: _FAKE

    def __getitem__(self, k):
        return _FAKE


_FAKE = _Fake()


def _isfake(x):
    return isinstance(x, _Fake)


class Ctr:
    def __init__(self, name, sem):
        self.name = name
        self.sem = sem
        self.n = 0

    def inc(self, instr, by=1):
        if not _isfake(instr):
            instr.then_inc(self.sem, by)
        self.n += by
        return self.n


class Em:
    def __init__(self, ctrs, milestones, dry):
        self.c = ctrs
        self.M = milestones
        self.dry = dry
        self._waited = {}

    def mark(self, key, ctr):
        if self.dry:
            self.M[key] = ctr.n
        else:
            assert self.M.get(key) == ctr.n, (key, self.M.get(key), ctr.n)

    def wait(self, eng, ename, ctr, key=None, val=None):
        if self.dry:
            return
        v = self.M[key] if key is not None else val
        if v is None or v <= 0:
            return
        prev = self._waited.get((ename, ctr.name), 0)
        if v <= prev:
            return
        self._waited[(ename, ctr.name)] = v
        eng.wait_ge(ctr.sem, v)

    def selfwait(self, eng, ename, ctr):
        self.wait(eng, ename, ctr, val=ctr.n)

    def fence_capture(self):
        return {k: c.n for k, c in self.c.items()}

    def fence_wait(self, eng, ename, fence):
        for k, v in fence.items():
            if v and k in self.c:
                self.wait(eng, ename, self.c[k], val=v)


def _r3(t, groups, part=None):
    if _isfake(t):
        return t
    ap = t.ap() if hasattr(t, "ap") and not isinstance(t, bass.AP) else t
    ap = ap.rearrange("p (c u) -> p c u", c=groups)
    if part is not None:
        ap = ap[part[0]:part[1]]
    return ap


def build_program():
    nc = bass.Bass("TRN2", target_bir_lowering=False)

    x_h = nc.dram_tensor("x", [L, D], F32, kind="ExternalInput").ap()
    wqt_h = nc.dram_tensor("wqt", [D, D], F32, kind="ExternalInput").ap()
    wkt_h = nc.dram_tensor("wkt", [D, D], F32, kind="ExternalInput").ap()
    wvt_h = nc.dram_tensor("wvt", [D, D], F32, kind="ExternalInput").ap()
    wft_h = nc.dram_tensor("wft", [D, D], F32, kind="ExternalInput").ap()
    bqr_h = nc.dram_tensor("bqr", [128, 8], F32, kind="ExternalInput").ap()
    bkr_h = nc.dram_tensor("bkr", [128, 8], F32, kind="ExternalInput").ap()
    bvb_h = nc.dram_tensor("bvb", [128, D], F32, kind="ExternalInput").ap()
    d2_h = nc.dram_tensor("d2", [128, D], F32, kind="ExternalInput").ap()
    eye_h = nc.dram_tensor("eye", [128, 128], BF16, kind="ExternalInput").ap()
    e7_h = nc.dram_tensor("e7", [128, 128], BF16, kind="ExternalInput").ap()
    e15_h = nc.dram_tensor("e15", [128, 128], BF16, kind="ExternalInput").ap()
    e13_h = nc.dram_tensor("e13", [128, 128], BF16, kind="ExternalInput").ap()
    pool_h = nc.dram_tensor("pool", [128, 4], BF16, kind="ExternalInput").ap()
    ones_h = nc.dram_tensor("ones1", [128, 1], BF16, kind="ExternalInput").ap()
    out_h = nc.dram_tensor("out", [L, D], F32, kind="ExternalOutput").ap()
    cc_in = nc.dram_tensor("cc_in", [1, 1], F32)
    cc_out = nc.dram_tensor("cc_out", [1, 1], F32)
    cc_w0 = nc.dram_tensor("cc_w0", [1, 1], F32)
    cc_w1 = nc.dram_tensor("cc_w1", [1, 1], F32)

    es = ExitStack()
    ctrs = {}

    def mksem(name):
        c = Ctr(name, es.enter_context(nc.semaphore(name)))
        ctrs[name] = c
        return c

    sg = mksem("sg")
    ss = mksem("ss")
    st = mksem("st")
    sv = mksem("sv")
    sa = mksem("sa")
    scc = mksem("scc")
    so = mksem("so")
    sp = mksem("sp")

    M = {}
    H = types.SimpleNamespace()

    OUTER = [("warm0", [1, 1], F32), ("qt", [128, 8 * L], BF16), ("eye", [128, 128], BF16),
             ("e7", [128, 128], BF16), ("e15", [128, 128], BF16),
             ("e13", [128, 128], BF16), ("pool", [128, 4], BF16),
             ("ones1", [128, 1], BF16), ("bq_r", [128, 8], F32),
             ("bk_r", [128, 8], F32), ("bqs", [128, 8], F32),
             ("bks", [128, 8], F32), ("qlm", [128, 1024], BF16),
             ("klm", [128, 1024], BF16), ("qlmp", [128, 2048], BF16),
             ("klmp", [128, 2048], BF16), ("w2aug", [128, NH * 65], BF16),
             ("vt0", [128, 2048], BF16), ("vt1", [128, 2048], BF16),
             ("bnorm", [128, 1024], BF16), ("gmax", [128, 1], F32),
             ("rg", [128, 1], F32), ("lmax", [1, 1], F32),
             ("cs_sb", [1, 4], F32), ("d_sb", [128, 16], F32),
             ("recd", [128, 16], F32)]
    CTX1 = [("kt", [128, 8 * L], BF16), ("xts", [128, 4096], BF16),
            ("xbf", [128, 4096], BF16), ("px", [128, 1024], BF16)]
    CTX2 = [("wq_sb", [128, 8 * D], BF16), ("wk_sb", [128, 8 * D], BF16)]
    PS_A = [("xtps0", [128, 1024], BF16), ("xtps1", [128, 1024], BF16),
            ("pxps", [128, 32], F32), ("qkps0", [128, 512], F32),
            ("qkps1", [128, 512], F32)]
    CTXB = [("k2sb", [128, 2048], BF16), ("k2t", [128, 2048], BF16),
            ("vc0", [128, 2048], BF16), ("vc1", [128, 2048], BF16),
            ("mtsb", [128, 2048], BF16), ("a1sb", [128, 2048], BF16),
            ("a2sb", [128, 2048], BF16), ("a3sb", [128, 2048], BF16),
            ("negmx2", [128, 16], F32), ("k2sum", [128, 16], F32),
            ("k2rec", [128, 16], F32)]
    PS_B = [("ps8", [128, 4096], F32)]
    CTXC = [("wv_sb", [128, 8 * D], BF16), ("vsb", [128, 2 * 1040], BF16),
            ("k3e", [128, 4096], BF16), ("bvb_sb", [128, D], F32)]
    PS_C = [("vps", [128, 512], F32), ("k3ps", [128, 1024], F32),
            ("bps0", [128, 390], F32), ("bps1", [128, 390], F32),
            ("bps2", [128, 260], F32)]
    CTXD = [("wft_sb", [128, 8 * D], BF16), ("d2_sb", [128, D], F32),
            ("wtmp", [128, D], F32), ("x32", [128, 2 * D], F32),
            ("k1e", [128, 3 * NH * 256], BF16),
            ("xn", [128, 2 * D], F32),
            ("r_sb", [128, 3 * D], F32), ("y32", [128, 3 * D], F32),
            ("y16", [128, 2 * D], BF16), ("yt_sb", [128, 2 * D], BF16),
            ("r2_sb", [128, 2 * D], F32), ("sq_sb", [128, D], F32),
            ("outst", [128, 2 * D], F32), ("recden", [128, 2 * 32], F32),
            ("stats", [128, 2 * 12], F32), ("mv1", [128, 2 * 2], F32),
            ("negm1", [128, 2], F32), ("veps1", [128, 2], F32),
            ("rstd1", [128, 2], F32),
            ("lnv", [128, 2], F32), ("s2", [128, 2 * 2], F32),
            ("s2t", [128, 2], F32), ("m2", [128, 2], F32),
            ("m2sq", [128, 2], F32), ("e2", [128, 2], F32),
            ("veps2", [128, 2], F32), ("negm2", [128, 2], F32),
            ("rstd2", [128, 2], F32), ("ssq", [128, 2], F32)]
    PS_D = [("k1psA", [128, 512], F32), ("k1psB", [128, 512], F32),
            ("xq0", [128, 390], F32), ("xq1", [128, 390], F32),
            ("xq2", [128, 260], F32),
            ("ytps", [128, 1024], BF16), ("zps", [128, 1024], F32)]

    ALL = OUTER + CTX1 + CTX2 + PS_A + CTXB + PS_B + CTXC + PS_C + CTXD + PS_D
    for nm, _, _ in ALL:
        setattr(H, nm, _FAKE)

    def alloc(stk, specs, psum=False):
        for nm, shape, dt in specs:
            f = nc.psum_tensor if psum else nc.sbuf_tensor
            setattr(H, nm, stk.enter_context(f("s_" + nm, shape, dt)))

    def phase_A(E, which, eng):
        g = s = t = v = a = eng
        if which == "s":
            for nm, src in [("eye", eye_h), ("e7", e7_h), ("e15", e15_h),
                            ("e13", e13_h), ("pool", pool_h), ("ones1", ones_h),
                            ("bq_r", bqr_h), ("bk_r", bkr_h)]:
                ss.inc(s.dma_start(out=getattr(H, nm)[:, :], in_=src[:, :]), 16)
            E.mark("consts_in", ss)

        if which == "g":
            sp.inc(g.memset(H.warm0[0:1, :], 0.0))
            E.selfwait(g, "g", sp)
            sg.inc(g.dma_start(out=cc_w0.ap(), in_=H.warm0[0:1, :]), 16)
            E.selfwait(g, "g", sg)
            scc.inc(g.collective_compute(
                "AllReduce", mybir.AluOpType.max,
                replica_groups=[list(range(8))],
                ins=[cc_w0.ap()], outs=[cc_w1.ap()]))
            for kc in range(8):
                sg.inc(g.dma_start(out=H.wq_sb[:, kc * D:(kc + 1) * D],
                                   in_=wqt_h[kc * 128:(kc + 1) * 128, :]), 16)
            E.mark("wq_in", sg)
            for i in range(4):
                sg.inc(g.dma_start(out=H.xbf[:, (i % 4) * D:(i % 4 + 1) * D],
                                   in_=x_h[i * 128:(i + 1) * 128, :]), 16)
                E.mark(("A_x_in", i), sg)
                E.selfwait(g, "g", sg)
            for i in range(4, NT):
                if i >= 4:
                    E.wait(g, "g", st, key=("A_px_mm", i - 4))
                sg.inc(g.dma_start(out=H.xbf[:, (i % 4) * D:(i % 4 + 1) * D],
                                   in_=x_h[i * 128:(i + 1) * 128, :]), 16)
                E.mark(("A_x_in", i), sg)
                E.selfwait(g, "g", sg)
                if i == 7:
                    for kc in range(8):
                        sg.inc(g.dma_start(
                            out=H.wk_sb[:, kc * D:(kc + 1) * D],
                            in_=wkt_h[kc * 128:(kc + 1) * 128, :]), 16)
                    E.mark("wk_in", sg)
                    E.selfwait(g, "g", sg)

        if which == "v":
            sv.inc(v.memset(H.qlmp[:, :], 0.0))
            sv.inc(v.memset(H.klmp[:, :], 0.0))
            E.mark("pads0", sv)
            E.wait(v, "v", ss, key="consts_in")
            sv.inc(v.tensor_scalar(out=H.bqs[:, :], in0=H.bq_r[:, :],
                                   scalar1=SCALE, scalar2=None, op0=ALU.mult))
            sv.inc(v.tensor_scalar(out=H.bks[:, :], in0=H.bk_r[:, :],
                                   scalar1=SCALE, scalar2=None, op0=ALU.mult))
            E.mark("bqs", sv)
            for i in range(NT):
                srp, j = i // 4, i % 4
                E.wait(v, "v", st, key=("A_xt_mm", i))
                if j == 0 and srp >= 1:
                    E.wait(v, "v", st, key=("A_qk_mm", srp - 1, 1, 7))
                dst = _r3(H.xts, 8)
                sv.inc(v.tensor_copy(dst[:, :, j * 128:(j + 1) * 128]
                                     if not E.dry else dst,
                                     _r3(getattr(H, f"xtps{i % 2}"), 8)))
                E.mark(("A_xt_ev", i), sv)
                E.wait(v, "v", st, key=("A_px_mm", i))
                pxd = _r3(H.px, 8)
                sv.inc(v.tensor_copy(pxd[:, :, 4 * i % 128:4 * i % 128 + 4]
                                     if not E.dry else pxd,
                                     H.pxps[:, :]))
                E.mark(("A_px_ev", i), sv)

        if which == "t":
            E.wait(t, "t", ss, key="consts_in")
            qkbank = [0]
            for i in range(NT):
                srp, j = i // 4, i % 4
                E.wait(t, "t", sg, key=("A_x_in", i))
                if i >= 2:
                    E.wait(t, "t", sv, key=("A_xt_ev", i - 2))
                xb = H.xbf[:, (i % 4) * D:(i % 4 + 1) * D]
                xtp = getattr(H, f"xtps{i % 2}")
                for c in range(8):
                    mm = t.transpose(xtp[:, c * 128:(c + 1) * 128],
                                     xb[:, c * 128:(c + 1) * 128], H.eye[:, :])
                    if c == 7:
                        st.inc(mm)
                E.mark(("A_xt_mm", i), st)
                if i >= 1:
                    E.wait(t, "t", sv, key=("A_px_ev", i - 1))
                for c in range(8):
                    mm = t.matmul(H.pxps[:, c * 4:(c + 1) * 4],
                                  xb[:, c * 128:(c + 1) * 128], H.pool[:, :],
                                  start=True, stop=True, skip_group_check=True)
                    if c == 7:
                        st.inc(mm)
                E.mark(("A_px_mm", i), st)
                if j == 3:
                    E.wait(t, "t", sv, key=("A_xt_ev", i))
                    for ip, wsb in enumerate([H.wq_sb, H.wk_sb]):
                        E.wait(t, "t", sg,
                               key="wq_in" if ip == 0 else "wk_in")
                        for mc in range(8):
                            bank = getattr(H, f"qkps{qkbank[0] % 2}")
                            nq = qkbank[0]
                            if nq >= 2:
                                E.wait(t, "t", sa, val=E.M.get(("A_qk_ev_n", nq - 2)))
                            for kc in range(8):
                                mm = t.matmul(
                                    bank[:, :],
                                    wsb[:, kc * D + mc * 128:kc * D + (mc + 1) * 128],
                                    H.xts[:, kc * 512:(kc + 1) * 512],
                                    start=(kc == 0), stop=(kc == 7))
                                if kc == 7:
                                    st.inc(mm)
                            E.mark(("A_qk_mm", srp, ip, mc), st)
                            qkbank[0] += 1
            E.wait(t, "t", sv, key=("A_px_ev", NT - 1))
            for ip, wsb in enumerate([H.wq_sb, H.wk_sb]):
                for mc in range(8):
                    bank = getattr(H, f"qkps{qkbank[0] % 2}")
                    nq = qkbank[0]
                    E.wait(t, "t", sa, val=E.M.get(("A_qk_ev_n", nq - 2)))
                    for kc in range(8):
                        mm = t.matmul(bank[:, 0:128],
                                      wsb[:, kc * D + mc * 128:kc * D + (mc + 1) * 128],
                                      H.px[:, kc * 128:(kc + 1) * 128],
                                      start=(kc == 0), stop=(kc == 7))
                        if kc == 7:
                            st.inc(mm)
                    E.mark(("A_lm_mm", ip, mc), st)
                    qkbank[0] += 1

        if which == "a":
            E.wait(a, "a", sv, key="bqs")
            nq = 0
            for srp in range(NSTRIP):
                for ip, (dst, bias) in enumerate([(H.qt, H.bqs), (H.kt, H.bks)]):
                    for mc in range(8):
                        bank = getattr(H, f"qkps{nq % 2}")
                        E.wait(a, "a", st, key=("A_qk_mm", srp, ip, mc))
                        sa.inc(a.activation(
                            out=dst[:, mc * L + srp * 512:mc * L + srp * 512 + 512],
                            in_=bank[:, :], func=AF.Identity,
                            bias=bias[:, mc:mc + 1], scale=SCALE))
                        E.mark(("A_qk_ev", srp, ip, mc), sa)
                        E.mark(("A_qk_ev_n", nq), sa)
                        nq += 1
            for ip, (dst, bias) in enumerate([(H.qlm, H.bqs), (H.klm, H.bks)]):
                for mc in range(8):
                    bank = getattr(H, f"qkps{nq % 2}")
                    E.wait(a, "a", st, key=("A_lm_mm", ip, mc))
                    sa.inc(a.activation(
                        out=dst[:, mc * 128:(mc + 1) * 128],
                        in_=bank[:, 0:128], func=AF.Identity,
                        bias=bias[:, mc:mc + 1], scale=SCALE))
                    E.mark(("A_lm_ev", ip, mc), sa)
                    E.mark(("A_qk_ev_n", nq), sa)
                    nq += 1

    def phase_B(E, which, eng, fence):
        g = s = t = v = a = eng

        def sl(i):
            return H.ps8[:, i * 512:(i + 1) * 512]

        def slg(i, off, n):
            return H.ps8[:, i * 512 + off:i * 512 + off + n]

        if which == "v":
            E.fence_wait(v, "v", fence)
            E.wait(v, "v", sv, key="pads0")
            for dst, srcq in [(H.qlmp, H.qlm), (H.klmp, H.klm)]:
                d3 = _r3(dst, 8, (0, 64))
                s3 = _r3(srcq, 8, (0, 64))
                sv.inc(v.tensor_copy(d3[:, :, 0:128] if not E.dry else d3, s3))
                d3 = _r3(dst, 8, (64, 128))
                s3 = _r3(srcq, 8, (64, 128))
                sv.inc(v.tensor_copy(d3[:, :, 128:256] if not E.dry else d3, s3))
            E.mark("pads", sv)
            for w in range(2):
                E.wait(v, "v", st, key=("B_k2_mm", w))
                bank = H.ps8[:, w * 2048:(w + 1) * 2048]
                for hh in range(8):
                    h = 8 * w + hh
                    sv.inc(v.tensor_reduce(H.negmx2[:, h:h + 1],
                                           bank[:, hh * 128:(hh + 1) * 128],
                                           axis=AX.X, op=ALU.max, negate=True))
                E.mark(("B_k2_mx", w), sv)
                E.wait(v, "v", sa, key=("B_k2_exp", w))
                sv.inc(v.reciprocal(H.k2rec[:, 8 * w:8 * w + 8],
                                    H.k2sum[:, 8 * w:8 * w + 8]))
                E.selfwait(v, "v", sv)
                for hh in range(8):
                    h = 8 * w + hh
                    sv.inc(v.tensor_scalar(
                        out=H.k2sb[:, h * 128:(h + 1) * 128],
                        in0=bank[:, hh * 128:(hh + 1) * 128],
                        scalar1=H.k2rec[:, h:h + 1], scalar2=None, op0=ALU.mult))
                E.mark(("B_k2_ev", w), sv)
            E.wait(v, "v", st, key="B_cs_mm")
            sv.inc(v.tensor_reduce(H.lmax[0:1, :], H.ps8[0:1, :2048],
                                   axis=AX.X, op=ALU.max))
            E.mark("lmax", sv)
            for w in range(2):
                E.wait(v, "v", st, key=("B_k2t_mm", w))
                sv.inc(v.tensor_copy(H.k2t[:, w * 1024:(w + 1) * 1024],
                                     H.ps8[:, 2048:3072]))
                E.mark(("B_k2t_ev", w), sv)
            E.wait(v, "v", sg, key="gmax_in")
            sv.inc(v.reciprocal(H.rg[:, :], H.gmax[:, :]))
            E.selfwait(v, "v", sv)
            sv.inc(v.tensor_scalar(out=H.vc0[:, :], in0=H.k2t[:, :],
                                   scalar1=H.rg[:, 0:1], scalar2=None,
                                   op0=ALU.mult))
            sv.inc(v.tensor_scalar(out=H.vt0[:, :], in0=H.k2sb[:, :],
                                   scalar1=H.rg[:, 0:1], scalar2=None,
                                   op0=ALU.mult))
            E.mark("v0", sv)
            for it in range(6):
                for grp in range(4):
                    E.wait(v, "v", st, key=("B_nm", it, grp, "M"))
                    sv.inc(v.scalar_tensor_tensor(
                        out=_r3(H.a1sb, 16)[:, grp * 4:(grp + 1) * 4]
                        if not E.dry else _FAKE,
                        in0=_r3(sl(grp), 4) if not E.dry else _FAKE,
                        scalar=-1.0, in1=_brc(H.e7),
                        op0=ALU.mult, op1=ALU.add))
                    E.mark(("B_a1", it, grp), sv)
                for grp in range(4):
                    E.wait(v, "v", st, key=("B_nm", it, grp, "P1"))
                    sv.inc(v.scalar_tensor_tensor(
                        out=_r3(H.a2sb, 16)[:, grp * 4:(grp + 1) * 4]
                        if not E.dry else _FAKE,
                        in0=_r3(sl(grp), 4) if not E.dry else _FAKE,
                        scalar=-1.0, in1=_brc(H.e15),
                        op0=ALU.mult, op1=ALU.add))
                    E.mark(("B_a2", it, grp), sv)
                for grp in range(4):
                    E.wait(v, "v", st, key=("B_nm", it, grp, "P2"))
                    sv.inc(v.scalar_tensor_tensor(
                        out=_r3(H.a3sb, 16)[:, grp * 4:(grp + 1) * 4]
                        if not E.dry else _FAKE,
                        in0=_r3(sl(4 + grp), 4) if not E.dry else _FAKE,
                        scalar=-1.0, in1=_brc(H.e13),
                        op0=ALU.mult, op1=ALU.add))
                    E.mark(("B_a3", it, grp), sv)

        if which == "a":
            for w in range(2):
                bank = H.ps8[:, w * 2048:(w + 1) * 2048]
                E.wait(a, "a", sv, key=("B_k2_mx", w))
                for hh in range(8):
                    h = 8 * w + hh
                    sa.inc(a.activation(
                        out=bank[:, hh * 128:(hh + 1) * 128],
                        in_=bank[:, hh * 128:(hh + 1) * 128],
                        func=AF.Exp, bias=H.negmx2[:, h:h + 1], scale=1.0,
                        accum_out=H.k2sum[:, h:h + 1]))
                E.mark(("B_k2_exp", w), sa)
            for it in range(6):
                vcn = [H.vc0, H.vc1][(it + 1) % 2]
                vtn = [H.vt0, H.vt1][(it + 1) % 2]
                for grp in range(4):
                    E.wait(a, "a", st, key=("B_nm", it, grp, "MT"))
                    sa.inc(a.activation(out=H.mtsb[:, grp * 512:(grp + 1) * 512],
                                        in_=sl(4 + grp)[:, :],
                                        func=AF.Identity))
                    E.mark(("B_mt", it, grp), sa)
                for grp in range(4):
                    cols = slice(grp * 512, (grp + 1) * 512)
                    E.wait(a, "a", st, key=("B_nm", it, grp, "Vn"))
                    sa.inc(a.activation(out=vcn[:, cols], in_=sl(grp)[:, :],
                                        func=AF.Identity, scale=0.25))
                    E.mark(("B_vn", it, grp), sa)
                for grp in range(4):
                    cols = slice(grp * 512, (grp + 1) * 512)
                    E.wait(a, "a", st, key=("B_nm", it, grp, "VnT"))
                    sa.inc(a.activation(out=vtn[:, cols], in_=sl(4 + grp)[:, :],
                                        func=AF.Identity, scale=0.25))
                    E.mark(("B_vnt", it, grp), sa)

        if which == "t":
            E.fence_wait(t, "t", fence)
            E.wait(t, "t", sv, key="pads")
            for w in range(2):
                bank = H.ps8[:, w * 2048:(w + 1) * 2048]
                for hh in range(8):
                    h = 8 * w + hh
                    c, hf = h // 2, h % 2
                    s2_ = slice(c * 256 + hf * 128, c * 256 + hf * 128 + 128)
                    mm = t.matmul(bank[:, hh * 128:(hh + 1) * 128],
                                  H.qlmp[:, s2_], H.klmp[:, s2_],
                                  start=True, stop=True, skip_group_check=True)
                    if hh == 7:
                        st.inc(mm)
                E.mark(("B_k2_mm", w), st)
            E.wait(t, "t", sv, key=("B_k2_ev", 1))
            mm = None
            for h in range(NH):
                mm = t.matmul(H.ps8[0:1, h * 128:(h + 1) * 128],
                              H.ones1[:, :], H.k2sb[:, h * 128:(h + 1) * 128],
                              start=True, stop=True, skip_group_check=True)
            st.inc(mm)
            E.mark("B_cs_mm", st)
            E.wait(t, "t", sv, key="lmax")
            for w in range(2):
                if w >= 1:
                    E.wait(t, "t", sv, key=("B_k2t_ev", 0))
                for hh in range(8):
                    h = 8 * w + hh
                    mm = t.matmul(H.ps8[:, 2048 + hh * 128:2048 + (hh + 1) * 128],
                                  H.k2sb[:, h * 128:(h + 1) * 128], H.eye[:, :],
                                  start=True, stop=True, skip_group_check=True)
                    if hh == 7:
                        st.inc(mm)
                E.mark(("B_k2t_mm", w), st)
            E.wait(t, "t", sv, key="v0")
            for it in range(6):
                vcc = [H.vc0, H.vc1][it % 2]
                vtc = [H.vt0, H.vt1][it % 2]
                for grp in range(4):
                    if it >= 1:
                        E.wait(t, "t", sa, key=("B_vn", it - 1, grp))
                        E.wait(t, "t", sv, key=("B_a1", it - 1, 3))
                    for hh in range(4):
                        h = grp * 4 + hh
                        mm = t.matmul(slg(grp, hh * 128, 128),
                                      H.k2t[:, h * 128:(h + 1) * 128],
                                      vcc[:, h * 128:(h + 1) * 128],
                                      start=True, stop=True, skip_group_check=True)
                        if hh == 3:
                            st.inc(mm)
                    E.mark(("B_nm", it, grp, "M"), st)
                for grp in range(4):
                    if it >= 1:
                        E.wait(t, "t", sa, key=("B_vnt", it - 1, grp))
                    for hh in range(4):
                        h = grp * 4 + hh
                        mm = t.matmul(slg(4 + grp, hh * 128, 128),
                                      vcc[:, h * 128:(h + 1) * 128],
                                      H.k2t[:, h * 128:(h + 1) * 128],
                                      start=True, stop=True, skip_group_check=True)
                        if hh == 3:
                            st.inc(mm)
                    E.mark(("B_nm", it, grp, "MT"), st)
                for grp in range(4):
                    E.wait(t, "t", sv, key=("B_a1", it, grp))
                    E.wait(t, "t", sa, key=("B_mt", it, grp))
                    for hh in range(4):
                        mm = t.matmul(slg(grp, hh * 128, 128),
                                      H.mtsb[:, grp * 512 + hh * 128:
                                             grp * 512 + (hh + 1) * 128],
                                      H.a1sb[:, grp * 512 + hh * 128:
                                             grp * 512 + (hh + 1) * 128],
                                      start=True, stop=True, skip_group_check=True)
                        if hh == 3:
                            st.inc(mm)
                    E.mark(("B_nm", it, grp, "P1"), st)
                for grp in range(4):
                    E.wait(t, "t", sv, key=("B_a2", it, grp))
                    for hh in range(4):
                        mm = t.matmul(slg(4 + grp, hh * 128, 128),
                                      H.mtsb[:, grp * 512 + hh * 128:
                                             grp * 512 + (hh + 1) * 128],
                                      H.a2sb[:, grp * 512 + hh * 128:
                                             grp * 512 + (hh + 1) * 128],
                                      start=True, stop=True, skip_group_check=True)
                        if hh == 3:
                            st.inc(mm)
                    E.mark(("B_nm", it, grp, "P2"), st)
                for grp in range(4):
                    E.wait(t, "t", sv, key=("B_a3", it, grp))
                    if it >= 1:
                        E.wait(t, "t", sa, key=("B_vnt", it - 1, grp))
                    for hh in range(4):
                        h = grp * 4 + hh
                        mm = t.matmul(slg(grp, hh * 128, 128),
                                      vtc[:, h * 128:(h + 1) * 128],
                                      H.a3sb[:, grp * 512 + hh * 128:
                                             grp * 512 + (hh + 1) * 128],
                                      start=True, stop=True, skip_group_check=True)
                        if hh == 3:
                            st.inc(mm)
                    E.mark(("B_nm", it, grp, "Vn"), st)
                for grp in range(4):
                    for hh in range(4):
                        h = grp * 4 + hh
                        mm = t.matmul(slg(4 + grp, hh * 128, 128),
                                      H.a3sb[:, grp * 512 + hh * 128:
                                             grp * 512 + (hh + 1) * 128],
                                      vtc[:, h * 128:(h + 1) * 128],
                                      start=True, stop=True, skip_group_check=True)
                        if hh == 3:
                            st.inc(mm)
                    E.mark(("B_nm", it, grp, "VnT"), st)

        if which == "g":
            E.wait(g, "g", sv, key="lmax")
            sg.inc(g.dma_start(out=cc_in.ap(), in_=H.lmax[0:1, :]), 16)
            E.selfwait(g, "g", sg)
            scc.inc(g.collective_compute(
                "AllReduce", mybir.AluOpType.max,
                replica_groups=[list(range(8))],
                ins=[cc_in.ap()], outs=[cc_out.ap()]))

    def _brc(eyet):
        if _isfake(eyet):
            return eyet
        return eyet.ap().unsqueeze(1).to_broadcast((128, 4, 128))

    def phase_C(E, which, eng, fence):
        g = s = t = v = a = eng
        LAGC = 3
        if which == "g":
            E.wait(g, "g", st, val=fence["st_A"])
            for i in range(4):
                sg.inc(g.dma_start(out=H.xbf[:, (i % 4) * D:(i % 4 + 1) * D],
                                   in_=x_h[i * 128:(i + 1) * 128, :]), 16)
                E.mark(("C_x_in", i), sg)
                E.selfwait(g, "g", sg)
            E.wait(g, "g", scc, val=2)
            sg.inc(g.dma_start(out=H.gmax[:, :],
                               in_=cc_out.ap().to_broadcast((128, 1))), 16)
            E.mark("gmax_in", sg)
            E.selfwait(g, "g", sg)
            E.wait(g, "g", st, val=fence["st"])
            for kc in range(8):
                sg.inc(g.dma_start(out=H.wv_sb[:, kc * D:(kc + 1) * D],
                                   in_=wvt_h[kc * 128:(kc + 1) * 128, :]), 16)
            E.mark("wv_in", sg)
            E.selfwait(g, "g", sg)
            for i in range(4, NT):
                if i >= 4:
                    E.wait(g, "g", st, key=("C_xt_mm", i - 4))
                sg.inc(g.dma_start(out=H.xbf[:, (i % 4) * D:(i % 4 + 1) * D],
                                   in_=x_h[i * 128:(i + 1) * 128, :]), 16)
                E.mark(("C_x_in", i), sg)
                E.selfwait(g, "g", sg)

        if which == "s":
            E.fence_wait(s, "s", fence)
            ss.inc(s.dma_start(out=H.bvb_sb[:, :], in_=bvb_h[:, :]), 16)
            E.mark("bvb_in", ss)

        if which == "t":
            E.fence_wait(t, "t", fence)
            E.wait(t, "t", ss, key="bvb_in")
            E.wait(t, "t", sg, key="wv_in")
            for p in range(NT + LAGC):
                if 0 <= p < NT:
                    E.wait(t, "t", sg, key=("C_x_in", p))
                    if p >= 2:
                        E.wait(t, "t", sv, key=("C_xt_ev", p - 2))
                    xb = H.xbf[:, (p % 4) * D:(p % 4 + 1) * D]
                    xtp = getattr(H, f"xtps{p % 2}")
                    for c in range(8):
                        mm = t.transpose(xtp[:, c * 128:(c + 1) * 128],
                                         xb[:, c * 128:(c + 1) * 128], H.eye[:, :])
                        if c == 7:
                            st.inc(mm)
                    E.mark(("C_xt_mm", p), st)
                i = p - 1
                if 0 <= i < NT:
                    xt_t = H.xts[:, (i % 2) * 1024:(i % 2 + 1) * 1024]
                    E.wait(t, "t", sv, key=("C_xt_ev", i))
                    if i >= 1:
                        E.wait(t, "t", sv, key=("C_v_ev", i - 1, 1))
                    for kc in range(8):
                        mm = t.matmul(
                            H.vps[:, :], xt_t[:, kc * 128:(kc + 1) * 128],
                            H.wv_sb[:, kc * D:kc * D + 512],
                            start=(kc == 0), stop=(kc == 7))
                        if kc == 7:
                            st.inc(mm)
                    E.mark(("C_v_mm", i, 0), st)
                    if i >= 1:
                        E.wait(t, "t", sa, key=("C_k3_ev", i - 1, 1))
                    for cp in range(4):
                        mm = t.matmul(H.k3ps[:, cp * 256:(cp + 1) * 256],
                                      H.kt[:, cp * L + i * 128:cp * L + i * 128 + 128],
                                      H.qlmp[:, cp * 256:(cp + 1) * 256],
                                      start=True, stop=True, skip_group_check=True)
                        if cp == 3:
                            st.inc(mm)
                    E.mark(("C_k3_mm", i, 0), st)
                    E.wait(t, "t", sv, key=("C_v_ev", i, 0))
                    for kc in range(8):
                        mm = t.matmul(
                            H.vps[:, :], xt_t[:, kc * 128:(kc + 1) * 128],
                            H.wv_sb[:, kc * D + 512:kc * D + 1024],
                            start=(kc == 0), stop=(kc == 7))
                        if kc == 7:
                            st.inc(mm)
                    E.mark(("C_v_mm", i, 1), st)
                    E.wait(t, "t", sa, key=("C_k3_ev", i, 0))
                    for cp in range(4):
                        c = 4 + cp
                        mm = t.matmul(H.k3ps[:, cp * 256:(cp + 1) * 256],
                                      H.kt[:, c * L + i * 128:c * L + i * 128 + 128],
                                      H.qlmp[:, c * 256:(c + 1) * 256],
                                      start=True, stop=True, skip_group_check=True)
                        if cp == 3:
                            st.inc(mm)
                    E.mark(("C_k3_mm", i, 1), st)
                i = p - 2
                if 0 <= i < NT:
                    for w in range(2):
                        E.wait(t, "t", sa, key=("C_k3_ev", i, w))
                        E.wait(t, "t", sv, key=("C_v_ev", i, 1))
                        if i == 0 and w == 0:
                            E.wait(t, "t", sv, key="vones")
                        mm2 = None
                        for hh in range(8):
                            h = 8 * w + hh
                            k3sl = H.k3e[:, (i % 2) * 2048 + h * 128:
                                         (i % 2) * 2048 + h * 128 + 128]
                            bk, hb = (h // 6, h % 6) if h < 12 else (2, h - 12)
                            bbank = getattr(H, f"bps{bk}")
                            mm2 = t.matmul(
                                bbank[:, hb * 65:hb * 65 + 65],
                                k3sl,
                                H.vsb[:, (i % 2) * 1040 + h * 65:
                                      (i % 2) * 1040 + h * 65 + 65],
                                start=(i == 0 and hb == 0),
                                stop=(i == NT - 1),
                                skip_group_check=True)
                        st.inc(mm2)
                        E.mark(("C_b_mm", i, w), st)
            E.wait(t, "t", sv, key="bnorm")
            for w in range(2):
                if w:
                    E.wait(t, "t", sv, key=("C_w2_ev", 0))
                mm = None
                for hh in range(8):
                    h = 8 * w + hh
                    mm = t.matmul(H.k3ps[:, hh * 64:(hh + 1) * 64],
                                  H.vt0[:, h * 128:(h + 1) * 128],
                                  H.bnorm[:, h * 64:(h + 1) * 64],
                                  start=(hh == 0), stop=True,
                                  skip_group_check=True)
                st.inc(mm)
                E.mark(("C_w2_mm", w), st)

        if which == "a":
            for p in range(NT + LAGC):
                i = p - 1
                if 0 <= i < NT:
                    for w in range(2):
                        E.wait(a, "a", st, key=("C_k3_mm", i, w))
                        if i >= 2:
                            E.wait(a, "a", st, key=("C_b_mm", i - 2, w))
                        sa.inc(a.activation(
                            out=H.k3e[:, (i % 2) * 2048 + w * 1024:
                                      (i % 2) * 2048 + w * 1024 + 1024],
                            in_=H.k3ps[:, :], func=AF.Exp))
                        E.mark(("C_k3_ev", i, w), sa)

        if which == "v":
            E.fence_wait(v, "v", fence)
            vd = _r3(H.vsb, 32)
            sv.inc(v.memset(vd[:, :, 64:65] if not E.dry else vd, 1.0))
            E.mark("vones", sv)
            for p in range(NT + LAGC):
                if 0 <= p < NT:
                    E.wait(v, "v", st, key=("C_xt_mm", p))
                    if p >= 2:
                        E.wait(v, "v", st, key=("C_v_mm", p - 2, 1))
                    sv.inc(v.tensor_copy(H.xts[:, (p % 2) * 1024:(p % 2 + 1) * 1024],
                                         getattr(H, f"xtps{p % 2}")[:, :]))
                    E.mark(("C_xt_ev", p), sv)
                i = p - 1
                if 0 <= i < NT:
                    for half in range(2):
                        E.wait(v, "v", st, key=("C_v_mm", i, half))
                        if i >= 2 and half == 0:
                            E.wait(v, "v", st, key=("C_b_mm", i - 2, 1))
                        vdst = _r3(H.vsb, 32)
                        sv.inc(v.scalar_tensor_tensor(
                            out=vdst[:, (i % 2) * 16 + half * 8:
                                     (i % 2) * 16 + (half + 1) * 8, 0:64]
                            if not E.dry else vdst,
                            in0=_r3(H.vps, 8) if not E.dry else _FAKE,
                            scalar=1.0,
                            in1=(H.bvb_sb[:, half * 512:half * 512 + 512]
                                 .rearrange("p (h n) -> p h n", h=8))
                            if not E.dry else _FAKE,
                            op0=ALU.mult, op1=ALU.add))
                        E.mark(("C_v_ev", i, half), sv)
            E.wait(v, "v", st, key=("C_b_mm", NT - 1, 1))
            for bk, (hh0, nh_) in enumerate([(0, 6), (6, 6), (12, 4)]):
                sv.inc(v.tensor_copy(
                    H.d_sb[:, hh0:hh0 + nh_].unsqueeze(-1)
                    if not E.dry else _FAKE,
                    _r3(getattr(H, f"bps{bk}"), nh_)[:, :, 64:65]
                    if not E.dry else _FAKE))
            E.selfwait(v, "v", sv)
            sv.inc(v.reciprocal(H.recd[:, :], H.d_sb[:, :]))
            E.selfwait(v, "v", sv)
            for h in range(NH):
                bk, hb = (h // 6, h % 6) if h < 12 else (2, h - 12)
                bbank = getattr(H, f"bps{bk}")
                sv.inc(v.tensor_scalar(
                    out=H.bnorm[:, h * 64:(h + 1) * 64],
                    in0=bbank[:, hb * 65:hb * 65 + 64],
                    scalar1=H.recd[:, h:h + 1], scalar2=None, op0=ALU.mult))
            E.mark("bnorm", sv)
            for w in range(2):
                E.wait(v, "v", st, key=("C_w2_mm", w))
                dst = _r3(H.w2aug, NH)
                sv.inc(v.tensor_copy(
                    dst[:, w * 8:(w + 1) * 8, 0:64] if not E.dry else dst,
                    _r3(H.k3ps, 16)[:, 0:8, :] if not E.dry else _FAKE))
                E.mark(("C_w2_ev", w), sv)
            dst = _r3(H.w2aug, NH)
            sv.inc(v.memset(dst[:, :, 64:65] if not E.dry else dst, 1.0))
            E.mark("w2ones", sv)

    def phase_D(E, which, eng, fence):
        g = s = t = v = a = eng
        LAG = 8
        G3 = [(0, 6), (6, 12), (12, 16)]

        if which == "s":
            E.fence_wait(s, "s", fence)
            ss.inc(s.dma_start(out=H.d2_sb[:, :], in_=d2_h[:, :]), 16)
            E.mark("d2_in", ss)
            E.selfwait(s, "s", ss)
            for kc in range(8):
                if kc >= 1:
                    E.wait(s, "s", sv, key=("D_wft", kc - 1))
                ss.inc(s.dma_start(out=H.wtmp[:, :],
                                   in_=wft_h[kc * 128:(kc + 1) * 128, :]), 16)
                E.mark(("D_wft_in", kc), ss)
                E.selfwait(s, "s", ss)
            for p in range(NT + LAG):
                i = p - 6
                if 0 <= i < NT:
                    E.wait(s, "s", sp, key=("D_out", i))
                    so.inc(s.dma_start(
                        out=out_h[i * 128:(i + 1) * 128, :],
                        in_=H.outst[:, (i % 2) * D:(i % 2 + 1) * D]), 16)
                    E.selfwait(s, "s", so)
            if not E.dry:
                s.wait_ge(so.sem, NT * 16)

        if which == "g":
            E.fence_wait(g, "g", fence)
            for i0 in range(2):
                sg.inc(g.dma_start(out=H.x32[:, (i0 % 2) * D:(i0 % 2 + 1) * D],
                                   in_=x_h[i0 * 128:(i0 + 1) * 128, :]), 16)
                E.mark(("D_x_in", i0), sg)
                E.selfwait(g, "g", sg)
            for p in range(NT + LAG):
                i = p - 2
                if 0 <= i < NT:
                    E.wait(g, "g", sa, key=("D_rstd1", i))
                    if i >= 3:
                        E.wait(g, "g", sv, key=("D_r2h", i - 3, 1))
                    E.selfwait(g, "g", sp)
                    sp.inc(g.tensor_scalar(
                        out=H.y32[:, (i % 3) * D:(i % 3 + 1) * D],
                        in0=H.r_sb[:, (i % 3) * D:(i % 3 + 1) * D],
                        scalar1=H.negm1[:, i % 2:i % 2 + 1],
                        scalar2=H.rstd1[:, i % 2:i % 2 + 1],
                        op0=ALU.add, op1=ALU.mult))
                    E.mark(("D_y32", i), sp)
                if 0 <= p < NT:
                    E.wait(g, "g", sv, key=("D_xn", p))
                    E.wait(g, "g", sg, key=("D_x_in", p))
                    E.selfwait(g, "g", sp)
                    sp.inc(g.tensor_tensor(
                        out=H.r_sb[:, (p % 3) * D:(p % 3 + 1) * D],
                        in0=H.xn[:, (p % 2) * D:(p % 2 + 1) * D],
                        in1=H.x32[:, (p % 2) * D:(p % 2 + 1) * D], op=ALU.add))
                    E.mark(("D_r", p), sp)
                i2 = p + 2
                if 2 <= i2 < NT:
                    E.wait(g, "g", sp, key=("D_r", i2 - 2))
                    sg.inc(g.dma_start(out=H.x32[:, (i2 % 2) * D:(i2 % 2 + 1) * D],
                                       in_=x_h[i2 * 128:(i2 + 1) * 128, :]), 16)
                    E.mark(("D_x_in", i2), sg)
                    E.selfwait(g, "g", sg)
                i = p - 5
                if 0 <= i < NT:
                    E.wait(g, "g", sa, key=("D_rstd2", i))
                    if i >= 2:
                        E.wait(g, "g", so, val=16 * (i - 1))
                    E.selfwait(g, "g", sp)
                    sp.inc(g.tensor_scalar(
                        out=H.outst[:, (i % 2) * D:(i % 2 + 1) * D],
                        in0=H.r2_sb[:, (i % 2) * D:(i % 2 + 1) * D],
                        scalar1=H.negm2[:, i % 2:i % 2 + 1],
                        scalar2=H.rstd2[:, i % 2:i % 2 + 1],
                        op0=ALU.add, op1=ALU.mult))
                    E.mark(("D_out", i), sp)

        if which == "t":
            E.wait(t, "t", sv, key=("C_xt_ev", NT - 1))

            def emit_k1h(s_, h):
                nb = s_ * 16 + h
                if nb >= 2:
                    pn = nb - 2
                    E.wait(t, "t", sa, key=("D_k1_ev", pn // 16, pn % 16))
                c, hf = h // 2, h % 2
                ksl = slice(c * 256 + hf * 128, c * 256 + hf * 128 + 128)
                bank = [H.k1psA, H.k1psB][nb % 2]
                mm = t.matmul(bank[:, :], H.klmp[:, ksl],
                              H.qt[:, c * L + s_ * 512:c * L + s_ * 512 + 512],
                              start=True, stop=True, skip_group_check=True)
                st.inc(mm)
                E.mark(("D_k1_mm", s_, h), st)

            def emit_x(p, g3):
                h0, h1 = G3[g3]
                s_, j = p // 4, p % 4
                E.wait(t, "t", sa, key=("D_k1_ev", s_, h1 - 1))
                if p >= 1:
                    E.wait(t, "t", sv, key=("D_xn", p - 1, g3))
                xqt = getattr(H, f"xq{g3}")
                hs = 2 * s_ + (j // 2)
                mm = None
                for idx, hh in enumerate(range(h0, h1)):
                    k1sl = H.k1e[:, (hs % 3) * 4096 + hh * 256 + (j % 2) * 128:
                                 (hs % 3) * 4096 + hh * 256 + (j % 2) * 128 + 128]
                    mm = t.matmul(xqt[:, idx * 65:idx * 65 + 65],
                                  k1sl, H.w2aug[:, hh * 65:hh * 65 + 65],
                                  start=(idx == 0), stop=True,
                                  skip_group_check=True)
                st.inc(mm)
                E.mark(("D_x_mm", p, g3), st)

            for h in range(16):
                emit_k1h(0, h)
            for p in range(NT + LAG):
                s_, j = p // 4, p % 4
                if 0 <= p < NT:
                    E.wait(t, "t", sv, key="w2ones")
                    for g3 in range(3):
                        if s_ + 1 < NSTRIP:
                            emit_k1h(s_ + 1, 4 * j + g3)
                        emit_x(p, g3)
                    if s_ + 1 < NSTRIP:
                        emit_k1h(s_ + 1, 4 * j + 3)
                i = p - 3
                if 0 <= i < NT:
                    E.wait(t, "t", sv, key=("D_y16", i))
                    if i >= 1:
                        E.wait(t, "t", sv, key=("D_yt_ev", i - 1))
                    for c in range(8):
                        mm = t.transpose(H.ytps[:, c * 128:(c + 1) * 128],
                                         H.y16[:, (i % 2) * D + c * 128:
                                               (i % 2) * D + (c + 1) * 128],
                                         H.eye[:, :])
                        if c == 7:
                            st.inc(mm)
                    E.mark(("D_yt_mm", i), st)
                i = p - 4
                if 0 <= i < NT:
                    E.wait(t, "t", sv, key=("D_yt_ev", i))
                    if i >= 1:
                        E.wait(t, "t", sv, key=("D_r2h", i - 1, 1))
                    for half in range(2):
                        for kc in range(8):
                            mm = t.matmul(
                                H.zps[:, half * 512:half * 512 + 512],
                                H.yt_sb[:, (i % 2) * D + kc * 128:
                                        (i % 2) * D + (kc + 1) * 128],
                                H.wft_sb[:, kc * D + half * 512:
                                         kc * D + half * 512 + 512],
                                start=(kc == 0), stop=(kc == 7))
                            if kc == 7:
                                st.inc(mm)
                        E.mark(("D_z_mm", i, half), st)

        if which == "a":
            E.wait(a, "a", st, key=("C_k3_mm", NT - 1, 1))

            def emit_k1e(s_, h):
                E.wait(a, "a", st, key=("D_k1_mm", s_, h))
                bank = [H.k1psA, H.k1psB][(s_ * 16 + h) % 2]
                for half in range(2):
                    b3 = (2 * s_ + half) % 3
                    sa.inc(a.activation(
                        out=H.k1e[:, b3 * 4096 + h * 256:
                                  b3 * 4096 + h * 256 + 256],
                        in_=bank[:, half * 256:half * 256 + 256], func=AF.Exp))
                E.mark(("D_k1_ev", s_, h), sa)

            for h in range(16):
                emit_k1e(0, h)
            for p in range(NT + LAG):
                s_, j = p // 4, p % 4
                if 0 <= p < NT and s_ + 1 < NSTRIP:
                    for g3 in range(4):
                        emit_k1e(s_ + 1, 4 * j + g3)
                i = p - 1
                if 0 <= i < NT:
                    E.wait(a, "a", sv, key=("D_st1", i))
                    E.selfwait(a, "a", sa)
                    sa.inc(a.activation(out=H.lnv[:, i % 2:i % 2 + 1],
                                        in_=H.veps1[:, i % 2:i % 2 + 1],
                                        func=AF.Ln))
                    E.selfwait(a, "a", sa)
                    sa.inc(a.activation(out=H.rstd1[:, i % 2:i % 2 + 1],
                                        in_=H.lnv[:, i % 2:i % 2 + 1],
                                        func=AF.Exp, scale=-0.5))
                    E.mark(("D_rstd1", i), sa)
                i = p - 4
                if 0 <= i < NT:
                    E.wait(a, "a", sv, key=("D_r2", i))
                    E.selfwait(a, "a", sa)
                    sa.inc(a.activation(out=H.sq_sb[:, :],
                                        in_=H.r2_sb[:, (i % 2) * D:(i % 2 + 1) * D],
                                        func=AF.Square,
                                        accum_out=H.ssq[:, i % 2:i % 2 + 1]))
                    E.mark(("D_ssq", i), sa)
                i = p - 5
                if 0 <= i < NT:
                    E.wait(a, "a", sv, key=("D_st2", i))
                    E.selfwait(a, "a", sa)
                    sa.inc(a.activation(out=H.lnv[:, i % 2:i % 2 + 1],
                                        in_=H.veps2[:, i % 2:i % 2 + 1],
                                        func=AF.Ln))
                    E.selfwait(a, "a", sa)
                    sa.inc(a.activation(out=H.rstd2[:, i % 2:i % 2 + 1],
                                        in_=H.lnv[:, i % 2:i % 2 + 1],
                                        func=AF.Exp, scale=-0.5))
                    E.mark(("D_rstd2", i), sa)

        if which == "v":
            E.fence_wait(v, "v", fence)
            E.wait(v, "v", ss, key="d2_in")
            for kc in range(8):
                E.wait(v, "v", ss, key=("D_wft_in", kc))
                sv.inc(v.tensor_tensor(out=H.wft_sb[:, kc * D:(kc + 1) * D],
                                       in0=H.wtmp[:, :],
                                       in1=H.d2_sb[:, :], op=ALU.add))
                E.mark(("D_wft", kc), sv)
            for p in range(NT + LAG):
                if 0 <= p < NT:
                    for g3, (h0, h1) in enumerate(G3):
                        ng = h1 - h0
                        E.wait(v, "v", st, key=("D_x_mm", p, g3))
                        xqt = getattr(H, f"xq{g3}")
                        rsl = H.recden[:, (p % 2) * 32 + h0:
                                       (p % 2) * 32 + h1]
                        sv.inc(v.reciprocal(
                            rsl.unsqueeze(-1) if not E.dry else _FAKE,
                            _r3(xqt, ng)[:, :, 64:65] if not E.dry else _FAKE))
                        E.selfwait(v, "v", sv)
                        sv.inc(v.tensor_tensor(
                            out=(H.xn[:, (p % 2) * D + h0 * 64:
                                      (p % 2) * D + h1 * 64]
                                 .rearrange("p (h n) -> p h n", h=ng))
                            if not E.dry else _FAKE,
                            in0=_r3(xqt, ng)[:, :, 0:64] if not E.dry else _FAKE,
                            in1=(rsl.unsqueeze(-1).to_broadcast((128, ng, 64)))
                            if not E.dry else _FAKE,
                            op=ALU.mult))
                        E.mark(("D_xn", p, g3), sv)
                    E.mark(("D_xn", p), sv)
                i = p - 2
                if 0 <= i < NT:
                    E.wait(v, "v", sa, key=("D_rstd1", i))
                    sv.inc(v.tensor_scalar(
                        out=H.y16[:, (i % 2) * D:(i % 2 + 1) * D],
                        in0=H.r_sb[:, (i % 3) * D:(i % 3 + 1) * D],
                        scalar1=H.negm1[:, i % 2:i % 2 + 1],
                        scalar2=H.rstd1[:, i % 2:i % 2 + 1],
                        op0=ALU.add, op1=ALU.mult))
                    E.mark(("D_y16", i), sv)
                i = p - 1
                if 0 <= i < NT:
                    E.wait(v, "v", sp, key=("D_r", i))
                    v.bn_stats(H.stats[:, (i % 2) * 12:(i % 2) * 12 + 6],
                               H.r_sb[:, (i % 3) * D:(i % 3) * D + 512])
                    sv.inc(v.bn_stats(H.stats[:, (i % 2) * 12 + 6:(i % 2) * 12 + 12],
                                      H.r_sb[:, (i % 3) * D + 512:(i % 3 + 1) * D]))
                    E.selfwait(v, "v", sv)
                    sv.inc(v.bn_aggr(H.mv1[:, (i % 2) * 2:(i % 2) * 2 + 2],
                                     H.stats[:, (i % 2) * 12:(i % 2 + 1) * 12]))
                    E.selfwait(v, "v", sv)
                    v.tensor_scalar(out=H.negm1[:, i % 2:i % 2 + 1],
                                    in0=H.mv1[:, (i % 2) * 2:(i % 2) * 2 + 1],
                                    scalar1=-1.0, scalar2=None, op0=ALU.mult)
                    sv.inc(v.tensor_scalar(
                        out=H.veps1[:, i % 2:i % 2 + 1],
                        in0=H.mv1[:, (i % 2) * 2 + 1:(i % 2) * 2 + 2],
                        scalar1=EPS, scalar2=None, op0=ALU.add))
                    E.mark(("D_st1", i), sv)
                i = p - 3
                if 0 <= i < NT:
                    E.wait(v, "v", st, key=("D_yt_mm", i))
                    sv.inc(v.tensor_copy(H.yt_sb[:, (i % 2) * D:(i % 2 + 1) * D],
                                         H.ytps[:, :]))
                    E.mark(("D_yt_ev", i), sv)
                i = p - 4
                if 0 <= i < NT:
                    E.wait(v, "v", sp, key=("D_y32", i))
                    if i >= 2:
                        E.wait(v, "v", sp, key=("D_out", i - 2))
                    for half in range(2):
                        E.wait(v, "v", st, key=("D_z_mm", i, half))
                        sv.inc(v.scalar_tensor_tensor(
                            out=H.r2_sb[:, (i % 2) * D + half * 512:
                                        (i % 2) * D + half * 512 + 512],
                            in0=H.zps[:, half * 512:half * 512 + 512],
                            scalar=1.0,
                            in1=H.y32[:, (i % 3) * D + half * 512:
                                      (i % 3) * D + half * 512 + 512],
                            op0=ALU.mult, op1=ALU.add,
                            accum_out=H.s2[:, (i % 2) * 2 + half:
                                           (i % 2) * 2 + half + 1]))
                        E.mark(("D_r2h", i, half), sv)
                    E.mark(("D_r2", i), sv)
                i = p - 5
                if 0 <= i < NT:
                    q = i % 2
                    E.wait(v, "v", sa, key=("D_ssq", i))
                    sv.inc(v.tensor_tensor(out=H.s2t[:, q:q + 1],
                                           in0=H.s2[:, q * 2:q * 2 + 1],
                                           in1=H.s2[:, q * 2 + 1:q * 2 + 2],
                                           op=ALU.add))
                    E.selfwait(v, "v", sv)
                    sv.inc(v.tensor_scalar(out=H.m2[:, q:q + 1],
                                           in0=H.s2t[:, q:q + 1],
                                           scalar1=R1024, scalar2=None,
                                           op0=ALU.mult))
                    E.selfwait(v, "v", sv)
                    v.tensor_scalar(out=H.negm2[:, q:q + 1], in0=H.m2[:, q:q + 1],
                                    scalar1=-1.0, scalar2=None, op0=ALU.mult)
                    v.tensor_tensor(out=H.m2sq[:, q:q + 1], in0=H.m2[:, q:q + 1],
                                    in1=H.m2[:, q:q + 1], op=ALU.mult)
                    sv.inc(v.tensor_scalar(out=H.e2[:, q:q + 1],
                                           in0=H.ssq[:, q:q + 1],
                                           scalar1=R1024, scalar2=None,
                                           op0=ALU.mult))
                    E.selfwait(v, "v", sv)
                    sv.inc(v.scalar_tensor_tensor(out=H.veps2[:, q:q + 1],
                                                  in0=H.m2sq[:, q:q + 1],
                                                  scalar=-1.0,
                                                  in1=H.e2[:, q:q + 1],
                                                  op0=ALU.mult, op1=ALU.add))
                    E.selfwait(v, "v", sv)
                    sv.inc(v.tensor_scalar(out=H.veps2[:, q:q + 1],
                                           in0=H.veps2[:, q:q + 1],
                                           scalar1=EPS, scalar2=None,
                                           op0=ALU.add))
                    E.mark(("D_st2", i), sv)

    ORDER = ["g", "s", "t", "v", "a"]

    def dry_run():
        for c in ctrs.values():
            c.n = 0
        E = Em(ctrs, M, dry=True)
        for w in ORDER:
            phase_A(E, w, _FAKE)
        fA = E.fence_capture()
        fA["st_A"] = fA["st"]
        for w in ORDER:
            phase_B(E, w, _FAKE, fA)
        fB = E.fence_capture()
        fB["st_A"] = fA["st"]
        for w in ORDER:
            phase_C(E, w, _FAKE, fB)
        fC = E.fence_capture()
        for w in ORDER:
            phase_D(E, w, _FAKE, fC)
        return fA, fB, fC

    fA, fB, fC = dry_run()

    for c in ctrs.values():
        c.n = 0
    E = Em(ctrs, M, dry=False)

    def emit_block(fn, *extra):
        with nc.Block() as blk:
            @blk.gpsimd
            def _(g):
                fn(E, "g", g, *extra)

            @blk.sync
            def _(s):
                fn(E, "s", s, *extra)

            @blk.tensor
            def _(t):
                fn(E, "t", t, *extra)

            @blk.vector
            def _(v):
                fn(E, "v", v, *extra)

            @blk.scalar
            def _(a):
                fn(E, "a", a, *extra)

    with es:
        alloc(es, OUTER)
        ctx1 = ExitStack()
        alloc(ctx1, CTX1)
        stA = ExitStack()
        alloc(stA, CTX2)
        alloc(stA, PS_A, psum=True)
        stA.close()
        stB = ExitStack()
        alloc(stB, CTXB)
        alloc(stB, PS_B, psum=True)
        stB.close()
        stC = ExitStack()
        alloc(stC, CTXC)
        alloc(stC, [("xtps0_c", [128, 1024], BF16),
                    ("xtps1_c", [128, 1024], BF16)], psum=True)
        alloc(stC, PS_C, psum=True)
        stC.close()
        ctx1.close()
        stD = ExitStack()
        alloc(stD, CTXD)
        alloc(stD, PS_D, psum=True)
        stD.close()

        xtps_a = (H.xtps0, H.xtps1)

        def run_all(E2, which, eng):
            H.xtps0, H.xtps1 = xtps_a
            phase_A(E2, which, eng)
            phase_B(E2, which, eng, fA)
            H.xtps0, H.xtps1 = H.xtps0_c, H.xtps1_c
            phase_C(E2, which, eng, fB)
            phase_D(E2, which, eng, fC)

        with nc.Block() as blk:
            @blk.gpsimd
            def _(g):
                run_all(E, "g", g)

            @blk.sync
            def _(s):
                run_all(E, "s", s)

            @blk.tensor
            def _(t):
                run_all(E, "t", t)

            @blk.vector
            def _(v):
                run_all(E, "v", v)

            @blk.scalar
            def _(a):
                run_all(E, "a", a)
    return nc


_NC_CACHE = {}


def kernel(L0_attn_x, L0_attn_Wq, L0_attn_bq, L0_attn_Wk, L0_attn_bk,
           L0_attn_Wv, L0_attn_bv, L0_ffn_weight, L0_ffn_disha):
    x = np.ascontiguousarray(np.asarray(L0_attn_x, dtype=np.float32))
    B = x.shape[0]
    wqt = np.ascontiguousarray(np.asarray(L0_attn_Wq, np.float32).T)
    wkt = np.ascontiguousarray(np.asarray(L0_attn_Wk, np.float32).T)
    wvt = np.ascontiguousarray(np.asarray(L0_attn_Wv, np.float32).T)
    wft = np.ascontiguousarray(np.asarray(L0_ffn_weight, np.float32).T)
    bq = np.asarray(L0_attn_bq, np.float32)
    bk = np.asarray(L0_attn_bk, np.float32)
    bv = np.asarray(L0_attn_bv, np.float32)
    dsh = np.asarray(L0_ffn_disha, np.float32)

    bqr = np.ascontiguousarray(bq.reshape(8, 128).T)
    bkr = np.ascontiguousarray(bk.reshape(8, 128).T)
    bvb = np.ascontiguousarray(np.broadcast_to(bv, (128, D)))
    d2 = np.ascontiguousarray(np.vstack([dsh, dsh]))
    eye = np.eye(128, dtype=ml_dtypes.bfloat16)
    e7 = (7.0 * np.eye(128)).astype(ml_dtypes.bfloat16)
    e15 = (15.0 * np.eye(128)).astype(ml_dtypes.bfloat16)
    e13 = (13.0 * np.eye(128)).astype(ml_dtypes.bfloat16)
    poolm = np.zeros((128, 4), dtype=ml_dtypes.bfloat16)
    for tt in range(128):
        poolm[tt, tt // 32] = 1.0 / 32.0
    ones1 = np.ones((128, 1), dtype=ml_dtypes.bfloat16)

    if "nc" not in _NC_CACHE:
        _NC_CACHE["nc"] = build_program()
    nc = _NC_CACHE["nc"]

    shared = dict(wqt=wqt, wkt=wkt, wvt=wvt, wft=wft, bqr=bqr, bkr=bkr,
                  bvb=bvb, d2=d2, eye=eye, e7=e7, e15=e15, e13=e13,
                  pool=poolm, ones1=ones1)
    in_maps = [dict(shared, x=np.ascontiguousarray(x[c])) for c in range(B)]
    res = run_bass_kernel_spmd(nc, in_maps, core_ids=list(range(B)))
    out = np.stack([np.asarray(res.results[c]["out"], np.float32)
                    for c in range(B)], axis=0)
    return out
